# version 30
# baseline (speedup 1.0000x reference)
import contextlib
import numpy as np
import concourse.bass as bass
import concourse.mybir as mybir
from concourse.bass_utils import run_bass_kernel_spmd

F32 = mybir.dt.float32
BF16 = mybir.dt.bfloat16
AF = mybir.ActivationFunctionType
ALU = mybir.AluOpType
AX = mybir.AxisListType

D = 1024
NH = 4
HD = 256
FFH = 4096
NEG_BIG = -3.0e38
SHIFT = 70.0
EPS = 1e-5
ALU_MIN = getattr(ALU, "min")


class Sem:
    __slots__ = ("h", "name", "n")

    def __init__(self, h, name):
        self.h, self.name, self.n = h, name, 0


class Plan:
    def __init__(self):
        self.ops = []
        self.record = {}

    def op(self, eng, emit, waits=(), inc=None, inck=1, ms=None):
        for w in waits:
            if isinstance(w, str) and w not in self.record:
                raise KeyError(f"undefined milestone {w!r} ({eng})")
        self.ops.append(dict(eng=eng, emit=emit, waits=list(waits), inc=inc, inck=inck))
        if inc is not None:
            inc.n += inck
            if ms is not None:
                self.record[ms] = (inc, inc.n)
        else:
            assert ms is None

    def emit_engine(self, engname, eng):
        hw = {}
        for o in self.ops:
            if o["eng"] != engname:
                continue
            for w in o["waits"]:
                sem, val = self.record[w] if isinstance(w, str) else w
                if hw.get(sem.name, -1) < val:
                    eng.wait_ge(sem.h, val)
                    hw[sem.name] = val
            inst = o["emit"](eng)
            if o["inc"] is not None:
                inst.then_inc(o["inc"].h, o["inck"])


def build(seq=8192, taps=()):
    T = seq // 128
    NQ = seq // 512
    OWN = seq // 4
    GSZ = min(512, OWN)
    OG = OWN // GSZ
    GT = GSZ // 128
    assert seq % 512 == 0 and GT % 2 == 0, (seq, GT)

    nc = bass.Bass("TRN2", target_bir_lowering=False)

    din = lambda name, shape: nc.dram_tensor(name, shape, F32, kind="ExternalInput").ap()
    x_seq = din("x_seq", [seq, D])
    x_own = din("x_own", [OWN, D])
    wqkv = din("wqkv", [768, D])
    woutT = din("woutT", [HD, D])
    w1 = din("w1", [FFH, D])
    w2 = din("w2", [D, FFH])
    w1t = din("w1t", [D, FFH])
    w2t = din("w2t", [FFH, D])
    gamma = din("gamma", [D])
    beta = din("beta", [D])
    b_out = din("b_out", [D])
    b1 = din("b1", [FFH])
    b2 = din("b2", [D])
    ident = din("ident", [128, 128])
    maskxneg = din("maskxneg", [128, 128])
    mask01 = din("mask01", [128, 128])
    y = nc.dram_tensor("y", [OWN, D], F32, kind="ExternalOutput").ap()

    def dscr(name, shape, dt):
        kind = "ExternalOutput" if name in taps else "Internal"
        return nc.dram_tensor(name, shape, dt, kind=kind).ap()

    attn_q = dscr("attn_q", [seq, HD], BF16)
    sg1d = dscr("sg1d", [D, FFH], BF16)
    sg2d = dscr("sg2d", [FFH, D], BF16)
    spart = dscr("spart", [seq, D], F32)
    sown = dscr("sown", [OWN, D], F32)

    P = Plan()
    es = contextlib.ExitStack()
    sb = lambda n, s, d: es.enter_context(nc.sbuf_tensor(n, s, d))
    mksem = lambda n: Sem(es.enter_context(nc.semaphore(n)), n)

    s_pe = mksem("s_pe")
    s_dve = mksem("s_dve")
    s_act = mksem("s_act")
    s_cc = mksem("s_cc")

    class DmaPool:
        def __init__(self, tag, n, eng):
            self.sems = [mksem(f"{tag}{i}") for i in range(n)]
            self.n = n
            self.eng = eng
            self.idx = 0

        def issue(self, dst, src, ms, waits=()):
            j = self.idx
            self.idx += 1
            sem = self.sems[j % self.n]
            gate = [(sem, 16 * (j // self.n))] if j >= self.n else []
            P.op(self.eng, lambda e, d=dst, s=src: e.dma_start(out=d, in_=s),
                 list(waits) + gate, sem, 16, ms)

    ldpool = DmaPool("ldq", 8, "sync")
    stpool = DmaPool("stq", 6, "scalar")
    gppool = DmaPool("gpq", 4, "gpsimd")

    s_id = sb("s_id", [128, 128], F32)
    s_idb = sb("s_idb", [128, 128], BF16)
    s_onesb = sb("s_onesb", [1, 128], BF16)
    s_onescol = sb("s_onescol", [64, 1], BF16)
    s_b1fm = sb("s_b1fm", [128, 32], F32)
    s_b2fm = sb("s_b2fm", [128, 8], F32)
    s_stat = sb("s_stat", [128, 96], F32)
    s_al1 = sb("s_al1", [128, 32], F32)
    s_al2 = sb("s_al2", [128, 12], F32)
    s_eps = sb("s_eps", [128, 1], F32)
    es_attn = contextlib.ExitStack()
    sba = lambda n, s, d: es_attn.enter_context(nc.sbuf_tensor(n, s, d))
    KA = [sba(f"KA{h}", [65, seq], BF16) for h in range(NH)]
    QA = [sba(f"QA{h}", [65, seq], BF16) for h in range(NH)]
    VA = sba("VA", [128, NH * T * 65], BF16)
    B70N = sba("B70N", [128, NH * T], BF16)

    V = lambda emit, waits, ms=None: P.op("vector", emit, waits, s_dve, 1, ms)
    A_ = lambda emit, waits, ms=None: P.op("scalar", emit, waits, s_act, 1, ms)
    PE = lambda emit, waits, ms=None: P.op("tensor", emit, waits, s_pe, 1, ms)
    LD = lambda dst, src, ms, waits=(): ldpool.issue(dst, src, ms, waits)
    ST = lambda dst, src, ms, waits=(): stpool.issue(dst, src, ms, waits)

    es0 = contextlib.ExitStack()
    sb0 = lambda n, s, d: es0.enter_context(nc.sbuf_tensor(n, s, d))
    esp0 = contextlib.ExitStack()
    ps0 = lambda n, s, d: esp0.enter_context(nc.psum_tensor(n, s, d))

    WqkvT = sb0("WqkvT", [128, 48 * 128], BF16)
    s_bWfm = sb0("s_bWfm", [128, 6], F32)
    s_bWvA = sb0("s_bWvA", [1, 128], BF16)
    s_bWvB = sb0("s_bWvB", [1, 128], BF16)
    es0t = contextlib.ExitStack()
    sb0t = lambda n, s, d: es0t.enter_context(nc.sbuf_tensor(n, s, d))
    s_wqt = [sb0t(f"s_wqt{i}", [128, 128], F32) for i in range(2)]
    s_wqtb = [sb0t(f"s_wqtb{i}", [128, 128], BF16) for i in range(2)]
    s_g8 = sb0t("s_g8", [8, 128], F32)
    s_b8 = sb0t("s_b8", [8, 128], F32)
    s_b1n = sb0t("s_b1n", [32, 128], F32)
    s_b2n = sb0t("s_b2n", [8, 128], F32)
    s_gfm = sb0t("s_gfm", [128, 8], F32)
    s_bfm = sb0t("s_bfm", [128, 8], F32)
    s_bfmb = sb0t("s_bfmb", [128, 8], BF16)
    s_m01f = sb0t("s_m01f", [128, 128], F32)
    p0m = ps0("p0m", [128, 512], F32)
    p0b = [ps0(f"p0b{i}", [128, 1024], BF16) for i in range(2)]
    p0w = ps0("p0w", [128, 16], F32)

    LD(s_id[:, :], ident, "ld_id")
    LD(s_g8[:, :], gamma.rearrange("(c p) -> c p", p=128), "ld_g")
    LD(s_b8[:, :], beta.rearrange("(c p) -> c p", p=128), "ld_b")
    LD(s_b1n[:, :], b1.rearrange("(c p) -> c p", p=128), "ld_b1")
    LD(s_b2n[:, :], b2.rearrange("(c p) -> c p", p=128), "ld_b2")

    V(lambda e: e.tensor_copy(s_idb[:, :], s_id[:, :]), ["ld_id"], "c_idb")
    V(lambda e: e.memset(s_onesb[:, :], 1.0), [], "c_ones")
    V(lambda e: e.memset(s_eps[:, :], float(EPS)), [], "c_eps")
    V(lambda e: e.memset(s_onescol[:, :], 1.0), [], "c_onescol")
    for h in range(NH):
        V(lambda e, h=h: e.memset(KA[h][64:65, :], 1.0), [], f"cka_{h}")
    V(lambda e: e.memset(VA[:, :].rearrange("p (s c) -> p s c", c=65)[:, :, 64:65], 1.0), [], "c_va1")

    PE(lambda e: e.transpose(p0m[:, 0:8], s_g8[:, :], s_id[0:8, 0:8]), ["ld_g", "ld_id"], "t_g")
    V(lambda e: e.tensor_copy(s_gfm[:, :], p0m[:, 0:8]), ["t_g"], "c_gfm")
    PE(lambda e: e.transpose(p0m[:, 8:16], s_b8[:, :], s_id[0:8, 0:8]), ["ld_b", "c_gfm"], "t_b")
    V(lambda e: e.tensor_copy(s_bfm[:, :], p0m[:, 8:16]), ["t_b"], "c_bfm")
    V(lambda e: e.tensor_copy(s_bfmb[:, :], s_bfm[:, :]), ["c_bfm"], "c_bfmb")
    PE(lambda e: e.transpose(p0m[:, 16:48], s_b1n[:, :], s_id[0:32, 0:32]), ["ld_b1", "c_bfm"], "t_b1")
    V(lambda e: e.tensor_copy(s_b1fm[:, :], p0m[:, 16:48]), ["t_b1"], "c_b1fm")
    PE(lambda e: e.transpose(p0m[:, 48:56], s_b2n[:, :], s_id[0:8, 0:8]), ["ld_b2", "c_b1fm"], "t_b2")
    V(lambda e: e.tensor_copy(s_b2fm[:, :], p0m[:, 48:56]), ["t_b2"], "c_b2fm")

    wq_chunk = lambda fc, dc: WqkvT[:, (fc * 8 + dc) * 128:(fc * 8 + dc) * 128 + 128]
    tiles06 = [(fc, dc) for fc in range(6) for dc in range(8)]
    for ki in range(48):
        fc, dc = tiles06[ki]
        gi = ki // 4
        j = ki % 4
        buf = p0b[gi % 2]
        w = [f"cwq_{ki-2}"] if ki >= 2 else []
        ldpool.issue(s_wqt[ki % 2][:, :], wqkv[fc * 128:(fc + 1) * 128, dc * 128:(dc + 1) * 128],
                     f"lwq_{ki}", w)
        V(lambda e, ki=ki: e.tensor_copy(s_wqtb[ki % 2][:, :], s_wqt[ki % 2][:, :]),
          [f"lwq_{ki}"] + ([f"t0_{(ki-2)//4}_{(ki-2)%4}"] if ki >= 2 else []), f"cwq_{ki}")
        PE(lambda e, ki=ki, gi=gi, j=j, buf=buf: e.transpose(
            buf[:, j * 128:(j + 1) * 128], s_wqtb[ki % 2][:, :], s_idb[:, :]),
            [f"cwq_{ki}", "c_idb"] + ([f"e0_{gi-2}"] if (j == 0 and gi >= 2) else []), f"t0_{gi}_{j}")
        if j == 3:
            V(lambda e, gi=gi, buf=buf: e.tensor_copy(
                WqkvT[:, :].rearrange("p (k c) -> p k c", c=128)[:, gi * 4:(gi + 1) * 4, :],
                buf[:, 0:512].rearrange("p (k c) -> p k c", c=128)),
                [f"t0_{gi}_{jj}" for jj in range(4)], f"e0_{gi}")
    e0_of = lambda fc, dc: f"e0_{(fc * 8 + dc) // 4}"

    for fc in range(6):
        for dc in range(8):
            PE(lambda e, fc=fc, dc=dc: e.matmul(p0w[:, fc:fc + 1], wq_chunk(fc, dc), s_bfmb[:, dc:dc + 1],
                                                start=(dc == 0), stop=(dc == 7)),
               [e0_of(fc, dc), "c_bfmb"], f"mmbw_{fc}" if dc == 7 else None)
    allbw = [f"mmbw_{fc}" for fc in range(6)]
    for fc in range(6):
        sc = 0.125 if fc < 2 else 1.0
        V(lambda e, fc=fc, sc=sc: e.tensor_scalar(s_bWfm[:, fc:fc + 1], p0w[:, fc:fc + 1], sc, None, op0=ALU.mult),
          allbw, f"cbw_{fc}")
    for i in range(2):
        PE(lambda e, i=i: e.transpose(p0m[0:1, 64 + i * 128:64 + (i + 1) * 128], s_bWfm[:, 4 + i:5 + i], s_id[:, :]),
           [f"cbw_{j}" for j in range(6)] + ["c_b2fm"], f"tbwv_{i}")
        V(lambda e, i=i: e.tensor_copy((s_bWvA if i == 0 else s_bWvB)[:, :],
                                       p0m[0:1, 64 + i * 128:64 + (i + 1) * 128]),
          [f"tbwv_{i}"], f"cbwv_{i}")
    for fc in range(6):
        for dc in range(8):
            if fc < 2:
                V(lambda e, fc=fc, dc=dc: e.tensor_scalar(wq_chunk(fc, dc), wq_chunk(fc, dc),
                                                          s_gfm[:, dc:dc + 1], 0.125, op0=ALU.mult, op1=ALU.mult),
                  ["c_gfm", "cbwv_0", "cbwv_1"] + allbw, f"scwq_{fc}_{dc}")
            else:
                V(lambda e, fc=fc, dc=dc: e.tensor_scalar(wq_chunk(fc, dc), wq_chunk(fc, dc),
                                                          s_gfm[:, dc:dc + 1], None, op0=ALU.mult),
                  ["c_gfm", "cbwv_0", "cbwv_1"] + allbw, f"scwq_{fc}_{dc}")
    wq_ready = [f"scwq_{fc}_{dc}" for fc in range(6) for dc in range(8)]

    esp0.close()
    es0t.close()
    ph0_done = ["scwq_5_7", "tbwv_1", "cbwv_1", "c_b1fm", "c_b2fm"]

    esp1 = contextlib.ExitStack()
    ps1 = lambda n, s, d: esp1.enter_context(nc.psum_tensor(n, s, d))
    s_x = [sb0(f"s_x{i}", [128, D], F32) for i in range(4)]
    s_a = [sb0(f"s_a{i}", [128, D], BF16) for i in range(4)]
    s_aT = sb0("s_aT", [128, 4 * D], BF16)
    p_tr = [ps1(f"p_tr{i}", [128, 1024], BF16) for i in range(2)]
    p_qk = [ps1(f"p_qk{i}", [128, 1024], F32) for i in range(2)]
    p_v = [ps1(f"p_v{i}", [128, 512], F32) for i in range(2)]

    def ln_group(pref, ntile, xbuf, abuf, xsrc, load_waits, abuf_waits, two_std, off, extra_first=(), jk_war=()):
        red = []
        for it in range(ntile):
            w = list(load_waits(it))
            if it >= 2:
                w = [f"{pref}sum{it-2}", f"{pref}sq{it-2}"]
            ldpool.issue(xbuf(it)[:, :], xsrc(it), f"{pref}x{it}", w)
            V(lambda e, it=it: e.reduce_sum(s_stat[:, off + it:off + it + 1], xbuf(it)[:, :], axis=AX.X),
              [f"{pref}x{it}"] + (list(extra_first) if it == 0 else []), f"{pref}sum{it}")
            A_(lambda e, it=it: e.activation(
                abuf(it)[:, :], xbuf(it)[:, :], AF.Square, bias=0.0, scale=1.0,
                accum_out=s_stat[:, off + 4 + it:off + 5 + it]),
                [f"{pref}x{it}"] + list(abuf_waits(it)), f"{pref}sq{it}")
            red += [f"{pref}sum{it}", f"{pref}sq{it}"]
        n = ntile
        sl = lambda c: s_stat[:, off + c:off + c + n]
        V(lambda e: e.tensor_scalar(sl(8), sl(0), 1.0 / D, None, op0=ALU.mult),
          red + list(extra_first), pref + "m")
        V(lambda e: e.tensor_scalar(sl(16), sl(4), 1.0 / D, None, op0=ALU.mult), red, pref + "e2")
        V(lambda e: e.tensor_tensor(sl(12), sl(8), sl(8), op=ALU.mult), [pref + "m"], pref + "msq")
        V(lambda e: e.tensor_tensor(sl(16), sl(16), sl(12), op=ALU.subtract),
          [pref + "e2", pref + "msq"], pref + "v")
        A_(lambda e: e.activation(sl(20), sl(16), AF.Sqrt, bias=s_eps[:, :], scale=1.0), [pref + "v", "c_eps"], pref + "s1")
        V(lambda e: e.reciprocal(sl(24), sl(20)), [pref + "s1"], pref + "rstd")
        if two_std:
            V(lambda e: e.tensor_tensor(sl(28), sl(24), sl(24), op=ALU.mult), [pref + "rstd"], pref + "r2")
            V(lambda e: e.tensor_tensor(sl(28), sl(28), sl(16), op=ALU.mult), [pref + "r2", pref + "v"], pref + "v2")
            A_(lambda e: e.activation(sl(32), sl(28), AF.Sqrt, bias=s_eps[:, :], scale=1.0), [pref + "v2"], pref + "s2")
            V(lambda e: e.reciprocal(sl(36), sl(32)), [pref + "s2"], pref + "rstd2")
            V(lambda e: e.tensor_tensor(sl(40), sl(24), sl(36), op=ALU.mult),
              [pref + "rstd2", pref + "rstd"], pref + "sc")
            scc = 40
        else:
            scc = 24
        for it in range(ntile):
            V(lambda e, it=it: e.tensor_scalar(
                abuf(it)[:, :], xbuf(it)[:, :], s_stat[:, off + 8 + it:off + 9 + it],
                s_stat[:, off + scc + it:off + scc + it + 1], op0=ALU.subtract, op1=ALU.mult),
                [pref + ("sc" if two_std else "rstd")] + list(abuf_waits(it)), f"{pref}a{it}")

    NG1 = T // 4

    def ln1(g4):
        pref = f"l1_{g4}_"
        pprev = f"l1_{g4-1}_"
        lw = lambda it, g4=g4: ([f"{pprev}a{it}"] if g4 >= 1 else list(ph0_done))
        aw = lambda it, g4=g4: ([f"tr_{(g4-1)*4+it}_7"] if g4 >= 1 else [])
        ln_group(pref, 4, lambda it: s_x[it], lambda it: s_a[it],
                 lambda it, g4=g4: x_seq[(g4 * 4 + it) * 128:(g4 * 4 + it + 1) * 128, :],
                 lw, aw, True, 48 * (g4 % 2),
                 extra_first=(["cbwv_1", "e0_11", "c_b2fm"] if g4 == 0 else
                              ([f"l1_{g4-2}_a{it}" for it in range(4)] if g4 >= 2 else [])),
                 jk_war=[f"{pprev}sq{3}"] if g4 >= 1 else [])

    ln1(0)
    for g4 in range(NG1):
        for it in range(4):
            t = g4 * 4 + it
            for dc in range(8):
                w = [f"l1_{g4}_a{it}", "c_idb"]
                if t == 0 and dc == 0:
                    w += ["cbwv_0", "cbwv_1", "e0_11"]
                if t >= 2:
                    w.append(f"aTe_{t-2}_1")
                if dc == 4:
                    w.append(f"aTe_{t}_0")
                PE(lambda e, t=t, it=it, dc=dc: e.transpose(
                    p_tr[t % 2][:, (dc % 4) * 128:(dc % 4) * 128 + 128],
                    s_a[it % 4][:, dc * 128:(dc + 1) * 128], s_idb[:, :]), w, f"tr_{t}_{dc}")
                if dc == 3 or dc == 7:
                    half = dc // 4
                    wv = [f"tr_{t}_{half*4+j}" for j in range(4)]
                    if g4 >= 1 and it == 0 and half == 0:
                        wv += [f"qk_{g4-1}_{fc}" for fc in range(4)] + [f"vm_{(g4-1)*4+j}" for j in range(4)]
                    V(lambda e, t=t, it=it, half=half: e.tensor_copy(
                        s_aT[:, it * D + half * 512: it * D + half * 512 + 512],
                        p_tr[t % 2][:, 0:512]), wv, f"aTe_{t}_{half}")
        if g4 + 1 < NG1:
            ln1(g4 + 1)
        aT = s_aT
        aTready = [f"aTe_{g4*4+it}_{h}" for it in range(4) for h in range(2)]
        for half in range(2):
            pq = p_qk[half]
            for fci in range(2):
                fc = half * 2 + fci
                for dc in range(8):
                    w = []
                    if dc == 0:
                        w = list(aTready)
                        if g4 == 0:
                            w += wq_ready
                        if g4 >= 1:
                            w += [f"qke_{g4-1}_{fc}_{s}" for s in range(2)]
                    PE(lambda e, fc=fc, dc=dc, fci=fci, pq=pq, aT=aT: e.matmul(
                        pq[:, fci * 512:fci * 512 + 512], wq_chunk(fc, dc),
                        aT[:, :].rearrange("p (t d) -> p t d", d=D)[:, :, dc * 128:dc * 128 + 128],
                        start=(dc == 0), stop=(dc == 7)),
                        w, f"qk_{g4}_{fc}" if dc == 7 else None)
            for fci in range(2):
                fc = half * 2 + fci
                for sub in range(2):
                    hh = (fc % 2) * 2 + sub
                    dst = (QA if fc < 2 else KA)[hh]
                    A_(lambda e, dst=dst, sub=sub, fci=fci, pq=pq, g4=g4, fc=fc: e.activation(
                        dst[0:64, g4 * 512:(g4 + 1) * 512],
                        pq[sub * 64:sub * 64 + 64, fci * 512:fci * 512 + 512],
                        AF.Identity, bias=s_bWfm[sub * 64:sub * 64 + 64, fc:fc + 1], scale=1.0),
                        [f"qk_{g4}_{fc}", f"cbw_{fc}"], f"qke_{g4}_{fc}_{sub}")
        for it in range(4):
            t = g4 * 4 + it
            pv = p_v[t % 2]
            for dc in range(8):
                w = []
                if dc == 0:
                    w = list(aTready) + ["cbwv_0", "cbwv_1"]
                    if t >= 2:
                        w += [f"ve_{t-2}_{s}" for s in range(4)]
                PE(lambda e, it=it, dc=dc, aT=aT, pv=pv: e.matmul(
                    pv[:, 0:256], aT[:, it * D + dc * 128: it * D + dc * 128 + 128],
                    WqkvT[:, :].rearrange("p (f k) -> p f k", f=6)[:, 4:6, dc * 128:dc * 128 + 128],
                    start=(dc == 0), stop=False), w, None)
            PE(lambda e, pv=pv: e.matmul(pv[:, 0:128], s_onesb[:, :], s_bWvA[:, :], start=False, stop=False),
               [], None)
            PE(lambda e, pv=pv: e.matmul(pv[:, 128:256], s_onesb[:, :], s_bWvB[:, :], start=False, stop=True),
               [], f"vm_{t}")
            for sub in range(4):
                V(lambda e, t=t, sub=sub, pv=pv: e.tensor_copy(
                    VA[:, (sub * T + t) * 65:(sub * T + t) * 65 + 64], pv[:, sub * 64:sub * 64 + 64]),
                    [f"vm_{t}", "c_va1"], f"ve_{t}_{sub}")
    p1_done = ([f"qke_{NG1-1}_{fc}_{s}" for fc in range(4) for s in range(2)] +
               [f"ve_{T-1}_{s}" for s in range(4)] + [f"ve_{T-2}_{s}" for s in range(4)])

    s_k2 = sb0("s_k2", [128, 64], F32)
    s_sc1 = sb0("s_sc1", [1, 4], F32)
    s_sc1b = sb0("s_sc1b", [1, 4], BF16)
    s_mkb = sb0("s_mkb", [128, 4], F32)
    NCH = (T + 7) // 8
    chs = [(c * 8, min(8, T - c * 8)) for c in range(NCH)]
    prev_end = ["cbw_0"]
    prev2 = ["cbw_0"]
    for h in range(NH):
        for side in range(2):
            src = (QA if side == 0 else KA)[h]
            for ch, (c0, nt) in enumerate(chs):
                w = list(p1_done) + list(prev_end)
                A_(lambda e, src=src, c0=c0, nt=nt: e.activation(
                    s_aT[0:64, 0:nt * 128], src[0:64, c0 * 128:(c0 + nt) * 128], AF.Square,
                    bias=0.0, scale=1.0), w, f"bsq_{h}_{side}_{ch}")
                ci = (h * 2 + side) * NCH + ch
                pv = p_v[ci % 2]
                for j in range(nt):
                    PE(lambda e, j=j, pv=pv: e.matmul(pv[:, 256 + j:257 + j],
                                               s_aT[0:64, j * 128:(j + 1) * 128], s_onescol[:, :],
                                               start=True, stop=True),
                       [f"bsq_{h}_{side}_{ch}", "c_onescol"] + list(prev2), f"bmm_{h}_{side}_{ch}_{j}")
                dst = (B70N[:, h * T + c0: h * T + c0 + nt] if side == 0
                       else s_k2[:, c0:c0 + nt])
                V(lambda e, dst=dst, nt=nt, pv=pv: e.tensor_copy(dst, pv[:, 256:256 + nt]),
                  [f"bmm_{h}_{side}_{ch}_{j}" for j in range(nt)], f"b2e_{h}_{side}_{ch}")
                prev2 = list(prev_end)
                prev_end = [f"b2e_{h}_{side}_{ch}"]
        V(lambda e, h=h: e.reduce_max(s_stat[:, 91:92], s_k2[:, 0:T], axis=AX.X),
          [f"b2e_{h}_1_{ch}" for ch in range(NCH)], f"bk1_{h}")
        PE(lambda e: e.transpose(p_qk[0][0:1, 0:128], s_stat[:, 91:92], s_id[:, :]),
           [f"bk1_{h}"] + list(p1_done) + ([f"bk6_{h-1}"] if h >= 1 else []), f"bk2_{h}")
        V(lambda e, h=h: e.reduce_max(s_sc1[:, h:h + 1], p_qk[0][0:1, 0:128], axis=AX.X), [f"bk2_{h}"], f"bk3_{h}")
        A_(lambda e, h=h: e.activation(s_sc1[:, h:h + 1], s_sc1[:, h:h + 1], AF.Sqrt, bias=0.0, scale=1.0),
           [f"bk3_{h}"], f"bk4s_{h}")
        V(lambda e, h=h: e.tensor_scalar(s_sc1b[:, h:h + 1], s_sc1[:, h:h + 1], -1.0, None, op0=ALU.mult),
          [f"bk4s_{h}"], f"bk4_{h}")
        PE(lambda e, h=h: e.matmul(p_qk[0][:, 128 + h:129 + h], s_onesb[:, :], s_sc1b[:, h:h + 1],
                                   start=True, stop=True), [f"bk4_{h}", f"bk3_{h}"], f"bk5_{h}")
        V(lambda e, h=h: e.tensor_copy(s_mkb[:, h:h + 1], p_qk[0][:, 128 + h:129 + h]), [f"bk5_{h}"], f"bk6_{h}")
        A_(lambda e, h=h: e.activation(B70N[:, h * T:(h + 1) * T], B70N[:, h * T:(h + 1) * T], AF.Sqrt,
                                       bias=0.0, scale=1.0),
           [f"b2e_{h}_0_{ch}" for ch in range(NCH)], f"bq1_{h}")
        V(lambda e, h=h: e.tensor_scalar(B70N[:, h * T:(h + 1) * T], B70N[:, h * T:(h + 1) * T],
                                         s_mkb[:, h:h + 1], SHIFT, op0=ALU.mult, op1=ALU.add),
          [f"bq1_{h}", f"bk6_{h}"], f"b70_{h}")
    b_ready = [f"b70_{h}" for h in range(NH)]

    es0.close()
    esp1.close()

    es2 = contextlib.ExitStack()
    sb2 = lambda n, s, d: es2.enter_context(nc.sbuf_tensor(n, s, d))
    ps2 = lambda n, s, d: es2.enter_context(nc.psum_tensor(n, s, d))
    PT = [sb2(f"PT{i}", [128, 1536], BF16) for i in range(3)]
    s_mxn = sb2("s_mxn", [128, 128], F32)
    s_m01 = sb2("s_m01", [128, 128], BF16)
    s_dj = sb2("s_dj", [128, 128], F32)
    s_dj2 = s_dj
    s_negL = sb2("s_negL", [128, 4], F32)
    s_negm = sb2("s_negm", [128, 4], F32)
    s_oT = [sb2("s_oT0", [65, 512], F32)]
    s_rl = sb2("s_rl", [128, 4], F32)
    s_ast = [sb2(f"s_ast{i}", [128, 256], BF16) for i in range(2)]
    WoT2 = sb2("WoT2", [128, 2 * D], BF16)
    s_astF2 = [sb2(f"s_astF2{i}", [128, 1024], BF16) for i in range(2)]
    s_sp = [sb2(f"s_sp{i}", [128, D], F32) for i in range(2)]
    s_boutr = sb2("s_boutr", [1, D], BF16)
    LD(s_mxn[:, :], maskxneg, "ld_mx", waits=list(p1_done))
    LD(s_dj[:, :], mask01, "ld_m01", waits=list(p1_done))
    V(lambda e: e.tensor_copy(s_m01[:, :], s_dj[:, :]), ["ld_m01"], "c_m01")
    p_sy = [ps2(f"p_sy{i}", [128, 1536], F32) for i in range(2)]
    for i_ in range(2):
        V(lambda e, i_=i_: e.memset(p_sy[i_][:, :], 0.0), list(p1_done), f"syz_{i_}")
    p_oa = [ps2(f"p_oa{i}", [128, 512], F32) for i in range(2)]
    p_dg_of = lambda p: p_oa[p % 2]
    p_ep_of = lambda p: p_oa[p % 2]

    s_wp = [sb2(f"s_wp{i}", [128, 1024], F32) for i in range(2)]
    for j_ in range(2):
        LD(s_wp[0][:, :], woutT[j_ * 128:(j_ + 1) * 128, :], f"ld_wot{j_}",
           waits=list(p1_done) + ([f"c_wot{j_-1}"] if j_ else []))
        V(lambda e, j_=j_: e.tensor_copy(WoT2[:, j_ * D:(j_ + 1) * D], s_wp[0][:, :]),
          [f"ld_wot{j_}"], f"c_wot{j_}")
    P.record["c_wot"] = P.record["c_wot1"]
    LD(s_wp[1][0:1, :], b_out.rearrange("(a d) -> a d", a=1), "ld_bo", waits=list(p1_done))
    V(lambda e: e.tensor_scalar(s_boutr[:, :], s_wp[1][0:1, :], 0.25, None, op0=ALU.mult), ["ld_bo"], "c_boutr")
    s_sgs = [sb2("s_sgs0", [128, 1024], BF16)]
    s_jkw = sb2("s_jkw", [128, 1024], BF16)
    s_sg2 = s_jkw

    prep_jobs = []
    prep_deferred = []
    pj = [0]

    s_gp = mksem("s_gp")

    def prep_alpha1(ti):
        j = pj[0]; pj[0] += 1
        w = [f"pj_{j-2}"] if j >= 2 else list(p1_done) + b_ready + ["c_wot1", "c_boutr"]
        ldpool.issue(s_wp[j % 2][:, :], w1[ti * 128:(ti + 1) * 128, :], f"pl_{j}", w)
        def fin(ti=ti, j=j):
            A_(lambda e, ti=ti, j=j: e.activation(s_jkw[:, :], s_wp[j % 2][:, :], AF.Abs, bias=0.0, scale=1.0,
                                                  accum_out=s_al1[:, ti:ti + 1]),
               [f"pl_{j}"] + ([f"pj_{j-1}"] if j >= 1 else []), f"pj_{j}")
        return fin

    def prep_alpha2(ti, qq_):
        j = pj[0]; pj[0] += 1
        w = [f"pj_{j-2}"] if j >= 2 else []
        ldpool.issue(s_wp[j % 2][:, :], w2[ti * 128:(ti + 1) * 128, qq_ * 1024:(qq_ + 1) * 1024], f"pl_{j}", w)
        def fin(ti=ti, qq_=qq_, j=j):
            A_(lambda e, ti=ti, qq_=qq_, j=j: e.activation(
                s_jkw[:, :], s_wp[j % 2][:, :], AF.Abs, bias=0.0, scale=1.0,
                accum_out=(s_al2[:, 8 + (qq_ % 2):9 + (qq_ % 2)] if qq_ > 0 else s_al2[:, ti:ti + 1])),
               [f"pl_{j}", f"pj_{j-1}"] + ([f"pa2_{ti}_{qq_-2}"] if qq_ >= 3 else []) +
               ([f"pa2_{ti-1}_{3}"] if (qq_ > 0 and ti >= 1) else []), f"pj_{j}")
            if qq_ > 0:
                V(lambda e, ti=ti, qq_=qq_: e.tensor_tensor(s_al2[:, ti:ti + 1], s_al2[:, ti:ti + 1],
                                                            s_al2[:, 8 + (qq_ % 2):9 + (qq_ % 2)], op=ALU.add),
                  [f"pj_{j}"] + ([f"pa2_{ti}_{qq_-1}"] if qq_ > 1 else []), f"pa2_{ti}_{qq_}")
        return fin

    sj_ctr = [0]

    def prep_sign(src_t, dst_img, r0, c0, nr, ncq):
        j = pj[0]; pj[0] += 1
        sj = sj_ctr[0]; sj_ctr[0] += 1
        w = [f"pj_{j-2}"] if j >= 2 else []
        ldpool.issue(s_wp[j % 2][:, 0:ncq], src_t[r0:r0 + nr, c0:c0 + ncq], f"pl_{j}", w)
        def fin(j=j, sj=sj, ncq=ncq):
            P.op("gpsimd", lambda e, j=j, sj=sj, ncq=ncq: e.tensor_scalar(
                s_sg2[:, 0:ncq], s_wp[j % 2][:, 0:ncq], 1.0e19, 1.0e19, op0=ALU.mult, op1=ALU.mult),
                [f"pl_{j}"] + ([f"pg2_{sj-1}"] if sj >= 1 else [f"pj_{j-1}"]), s_gp, 1, f"pg1_{sj}")
            P.op("gpsimd", lambda e, j=j, sj=sj, ncq=ncq: e.tensor_scalar(
                s_sgs[0][:, 0:ncq], s_sg2[:, 0:ncq], 1.0, -1.0, op0=getattr(ALU, "min"), op1=ALU.max),
                [f"pg1_{sj}"] + ([f"ps_{sj-1}"] if sj >= 1 else []), s_gp, 1, f"pg2_{sj}")
            P.record[f"pj_{j}"] = P.record[f"pg2_{sj}"]
            ST(dst_img[r0:r0 + nr, c0:c0 + ncq], s_sgs[0][:, 0:ncq], f"ps_{sj}", waits=[f"pg2_{sj}"])
        return fin

    for ti in range(32):
        prep_jobs.append(lambda ti=ti: prep_alpha1(ti))
    for ti in range(8):
        for qq_ in range(4):
            prep_jobs.append(lambda ti=ti, qq_=qq_: prep_alpha2(ti, qq_))
    for ti in range(8):
        for qq_ in range(4):
            prep_jobs.append(lambda ti=ti, qq_=qq_: prep_sign(w1t, sg1d, ti * 128, qq_ * 1024, 128, 1024))
    for ti in range(32):
        prep_jobs.append(lambda ti=ti: prep_sign(w2t, sg2d, ti * 128, 0, 128, 1024))

    if NQ % 4 == 0:
        TQORDER = [t for j in reversed(range(4)) for t in range(NQ) if t % 4 == j]
    else:
        TQORDER = list(reversed(range(NQ)))
    pairs = [(h, tq) for tq in TQORDER for h in range(NH)]
    NP = len(pairs)

    def batches_of(tq):
        out = []
        full = list(range(4 * tq))
        for i in range(0, len(full), 3):
            out.append([(kt, 512, 512 * j, 0) for j, kt in enumerate(full[i:i + 3])])
        out.append([(4 * tq + 0, 512, 0, 0), (4 * tq + 1, 384, 512, 128), (4 * tq + 2, 256, 1024, 256)])
        out.append([(4 * tq + 3, 128, 0, 384)])
        return out

    spm_hist = []
    spm_last = {}
    rs_pending = []
    rs_emitted = set()
    CH4 = seq // 4
    SW = max(seq // 16, 128)
    NCH = CH4 // SW
    chunk_of = lambda tq, c: ((tq * 512 + c * 128) % CH4) // SW

    def emit_spm_one(pos, c):
        tq = TQORDER[pos]
        p0 = 4 * pos
        bi = bi_g[0]
        bi_g[0] += 1
        fi = len(spm_hist)
        spm_hist.append((tq, c))
        w0 = [f"astfe_{p0 + hh}" for hh in range(4)] + ["c_wot", "c_boutr"]
        if bi >= 2:
            w0.append(f"exp_{bi-2}")
        first = [True]
        for dh2 in range(2):
            for hpair in range(2):
                PE(lambda e, bi=bi, c=c, dh2=dh2, hpair=hpair, pos=pos: e.matmul(
                    p_sy[bi % 2][:, dh2 * 512:(dh2 + 1) * 512],
                    s_astF2[pos % 2][:, hpair * 512 + c * 128: hpair * 512 + (c + 1) * 128],
                    WoT2[:, hpair * D + dh2 * 512: hpair * D + dh2 * 512 + 512],
                    start=(hpair == 0), stop=False, skip_group_check=True),
                    w0 if first[0] else [], None)
                first[0] = False
        for dh2 in range(2):
            PE(lambda e, bi=bi, dh2=dh2: e.matmul(
                p_sy[bi % 2][:, dh2 * 512:(dh2 + 1) * 512], s_onesb[:, :],
                s_boutr[:, dh2 * 512:(dh2 + 1) * 512],
                start=False, stop=True, skip_group_check=True),
                [], f"sy_{bi}" if dh2 == 1 else None)
        wv = [f"sy_{bi}"]
        if fi >= 2:
            pt, pc = spm_hist[fi - 2]
            wv.append(f"sps_{pt}_{pc}")
        V(lambda e, bi=bi, fi=fi: e.tensor_copy(s_sp[fi % 2][:, :], p_sy[bi % 2][:, 0:D]),
          wv, f"spv_{bi}")
        P.record[f"exp_{bi}"] = P.record[f"spv_{bi}"]
        P.record[f"avb_{bi}"] = P.record[f"avb_{bi-3}"] if bi >= 3 else P.record[f"spv_{bi}"]
        t0 = tq * 512 + c * 128
        qq_, wi = t0 // CH4, t0 % CH4
        jj_, ii = wi // SW, wi % SW
        row0 = jj_ * (4 * SW) + qq_ * SW + ii
        ST(spart[row0:row0 + 128, :], s_sp[fi % 2][:, :], f"sps_{tq}_{c}", waits=[f"spv_{bi}"])
        if c == 3:
            spm_last[pos] = f"sy_{bi}"
        done_tc = set(spm_hist)
        for jc in range(NCH):
            if jc in rs_emitted:
                continue
            need = [(t, cc) for t in range(NQ) for cc in range(4) if chunk_of(t, cc) == jc]
            if all(tc in done_tc for tc in need):
                rs_emitted.add(jc)
                wj = [f"sps_{t}_{cc}" for (t, cc) in need]
                if jc <= 1 and NCH >= 4:
                    rs_pending.append((jc, wj))
                    continue
                if jc == 0:
                    rs_pending.append((jc, wj))
                    continue
                P.op("gpsimd", lambda e, jc=jc: e.collective_compute(
                    "ReduceScatter", ALU.add, replica_groups=[[0, 1, 2, 3], [4, 5, 6, 7]],
                    ins=[spart[jc * 4 * SW:(jc + 1) * 4 * SW, :].opt()],
                    outs=[sown[jc * SW:(jc + 1) * SW, :].opt()]),
                    waits=wj, inc=s_cc, inck=1, ms=f"rs{jc}_done")

    def emit_diag(p):
        h, tq = pairs[p]
        for j in range(4):
            t = 4 * tq + j
            w = []
            if j == 0:
                w = ([f"astfe_{p-2}"] if p >= 2 else list(p1_done)) + [f"b70_{h}"]
            PE(lambda e, h=h, t=t, j=j, p=p: e.matmul(
                p_dg_of(p)[:, j * 128:(j + 1) * 128],
                QA[h][0:64, t * 128:(t + 1) * 128], KA[h][0:64, t * 128:(t + 1) * 128],
                start=True, stop=True), w, f"dg_{p}_{j}")

    def emit_mstats(p):
        h, tq = pairs[p]
        for j in range(4):
            w = [f"dg_{p}_{3}", "ld_mx", "c_m01"]
            if j >= 1:
                w.append(f"ttr_{p}_{j-1}")
            elif p >= 1:
                w.append(f"ttm_{p-1}")
            V(lambda e, j=j, p=p: e.tensor_tensor(s_dj[:, :], p_dg_of(p)[:, j * 128:(j + 1) * 128], s_mxn[:, :],
                                                  op=ALU.add), w, f"ttra_{p}_{j}")
            V(lambda e, j=j: e.tensor_scalar(s_dj2[:, :], s_dj[:, :], -1.0, None, op0=ALU.mult,
                                             op1=ALU_MIN, accum_out=s_negL[:, j:j + 1]),
              [f"ttra_{p}_{j}"], f"ttr_{p}_{j}")
        V(lambda e, h=h, tq=tq: e.tensor_tensor(s_negm[:, :], s_negL[:, :],
                                                B70N[:, h * T + 4 * tq: h * T + 4 * tq + 4], op=ALU_MIN),
          [f"ttr_{p}_{j}" for j in range(4)] + [f"b70_{h}"], f"ttm_{p}")

    def emit_mtrans(p):
        for j in range(4):
            PE(lambda e, j=j, p=p: e.transpose(p_dg_of(p)[0:1, j * 128:(j + 1) * 128], s_negm[:, j:j + 1],
                                               s_id[:, :]),
               [f"ttm_{p}"] + [f"ttr_{p}_{jj}" for jj in range(4)], f"mtr_{p}_{j}")
        h, tq = pairs[p]
        for j in range(4):
            V(lambda e, h=h, tq=tq, j=j, p=p: e.tensor_copy(
                QA[h][64:65, (4 * tq + j) * 128:(4 * tq + j) * 128 + 128],
                p_dg_of(p)[0:1, j * 128:(j + 1) * 128]),
                [f"mtr_{p}_{3}"], f"ngc_{p}_{j}")

    bi_g = [0]

    def emit_sweep(p, inj=None):
        inj = inj or {}
        h, tq = pairs[p]
        bs = batches_of(tq)
        nb = len(bs)
        base = bi_g[0]
        for b in range(nb + 1):
            bi = base + b
            cb = inj.pop(b, None) or (inj.pop(min(inj), None) if (b == nb and inj) else None)
            if cb is not None:
                cb()
            if b == nb and inj:
                for k in sorted(inj):
                    inj.pop(k)()
            if b < nb:
                batch = bs[b]
                w = [f"ngc_{p}_{j}" for j in range(4)]
                if bi >= 2:
                    w.append(f"exp_{bi-2}")
                last = len(batch) - 1
                for i, (kt, wd, off, qoff) in enumerate(batch):
                    PE(lambda e, h=h, tq=tq, kt=kt, wd=wd, off=off, qoff=qoff, bi=bi: e.matmul(
                        p_sy[bi % 2][:, off:off + wd],
                        KA[h][:, kt * 128:(kt + 1) * 128],
                        QA[h][:, tq * 512 + qoff: tq * 512 + qoff + wd],
                        start=True, stop=True),
                        w if i == 0 else [], f"sy_{bi}" if i == last else None)
                we = [f"sy_{bi}"]
                if bi >= 3:
                    we.append(f"avb_{bi-3}")
                wexp = max(off + wd for (kt, wd, off, qoff) in batch)
                A_(lambda e, bi=bi, wexp=wexp: e.activation(PT[bi % 3][:, 0:wexp], p_sy[bi % 2][:, 0:wexp],
                                                            AF.Exp, bias=0.0, scale=1.0),
                   we, f"exp_{bi}")
                if b >= nb - 2:
                    for i, (kt, wd, off, qoff) in enumerate(batch):
                        V(lambda e, bi=bi, off=off: e.tensor_tensor(
                            PT[bi % 3][:, off:off + 128], PT[bi % 3][:, off:off + 128], s_m01[:, :], op=ALU.mult),
                            [f"exp_{bi}", "c_m01"], f"msk_{bi}_{i}")
            if b > 0:
                bb = b - 1
                bi2 = base + bb
                batch = bs[bb]
                w = [f"exp_{bi2}"]
                if bb >= nb - 2:
                    w += [f"msk_{bi2}_{i}" for i in range(len(batch))]
                if bb == 0:
                    w.append(f"oev_{p-2}" if p >= 2 else "c_va1")
                    if p >= 2:
                        w.append(f"astfe_{p-2}")
                last = len(batch) - 1
                for i, (kt, wd, off, qoff) in enumerate(batch):
                    PE(lambda e, h=h, kt=kt, wd=wd, off=off, qoff=qoff, bi2=bi2, p=p,
                       st_=(bb == 0 and i == 0), sp_=(bb == nb - 1 and i == last): e.matmul(
                        p_oa[p % 2][0:65, qoff:qoff + wd],
                        VA[:, (h * T + kt) * 65:(h * T + kt) * 65 + 65],
                        PT[bi2 % 3][:, off:off + wd],
                        start=st_, stop=sp_, skip_group_check=True),
                        w if i == 0 else [], f"avb_{bi2}" if i == last else None)
        bi_g[0] += nb

    def emit_epi_start(p):
        V(lambda e, p=p: e.tensor_copy(s_oT[0][:, :], p_oa[p % 2][0:65, :]),
          [f"avb_{bi_g[0]-1}"] + ([f"etr_{p-1}_{3}"] if p >= 1 else []), f"oev_{p}")

    def emit_epi_tr(p):
        for j in range(4):
            w = [f"oev_{p}"]
            if p >= 2 and j == 0:
                w.append(f"astfe_{p-2}")
            PE(lambda e, j=j, p=p: e.transpose(p_ep_of(p)[:, j * 65:(j + 1) * 65],
                                               s_oT[0][:, j * 128:(j + 1) * 128],
                                               s_id[0:65, 0:65]), w, f"etr_{p}_{j}")

    def emit_epi_fin(p):
        h, tq = pairs[p]
        for j in range(4):
            V(lambda e, j=j, p=p: e.reciprocal(s_rl[:, j:j + 1], p_ep_of(p)[:, j * 65 + 64:j * 65 + 65]),
              [f"etr_{p}_{3}"] + ([f"rcp_{p}_{j-1}"] if j >= 1 else []), f"rcp_{p}_{j}")
        for j in range(4):
            w = [f"rcp_{p}_{j}"]
            if p >= 2 and j == 0:
                w.append(f"astf_{p-2}_{3}")
                if "attn_q" in taps:
                    w.append(f"ast_{p-2}")
            V(lambda e, p=p, j=j: e.tensor_scalar(
                s_ast[p % 2][:, j * 64:(j + 1) * 64], p_ep_of(p)[:, j * 65:j * 65 + 64],
                s_rl[:, j:j + 1], None, op0=ALU.mult), w, f"nrm_{p}_{j}")
        if "attn_q" in taps:
            ST(attn_q[tq * 512:(tq + 1) * 512, h * 64:(h + 1) * 64].rearrange("(j p) c -> p j c", p=128),
               s_ast[p % 2][:, :].rearrange("p (j c) -> p j c", c=64),
               f"ast_{p}", waits=[f"nrm_{p}_{j}" for j in range(4)])
        pos = p // 4
        for j in range(4):
            w = [f"nrm_{p}_{jj}" for jj in range(4)] if j == 0 else []
            PE(lambda e, j=j, p=p: e.matmul(
                p_ep_of(p)[0:64, j * 128:(j + 1) * 128],
                s_ast[p % 2][:, j * 64:(j + 1) * 64], s_idb[:, :],
                start=True, stop=True, skip_group_check=True), w, f"astf_{p}_{j}")
        wv = [f"astf_{p}_{3}"]
        if pos - 2 in spm_last:
            wv.append(spm_last[pos - 2])
        V(lambda e, p=p, h=h, pos=pos: e.tensor_copy(
            s_astF2[pos % 2][(h % 2) * 64:(h % 2) * 64 + 64, (h // 2) * 512:(h // 2) * 512 + 512],
            p_ep_of(p)[0:64, 0:512]), wv, f"astfe_{p}")

    emit_diag(0)
    emit_mstats(0)
    emit_mtrans(0)
    for p in range(NP):
        hh_, pos_ = p % 4, p // 4
        if hh_ == 1 and pos_ >= 2:
            emit_spm_one(pos_ - 2, 3)
        if hh_ >= 1 and pos_ >= 1:
            emit_spm_one(pos_ - 1, hh_ - 1)
        inj = {}
        if p >= 1:
            inj[1] = lambda p=p: (emit_epi_tr(p - 1), emit_epi_fin(p - 1))
        if p + 1 < NP:
            inj[2] = lambda p=p: (emit_diag(p + 1), emit_mstats(p + 1))
        emit_sweep(p, inj)
        for cb_ in prep_deferred:
            cb_()
        prep_deferred.clear()
        for _ in range(2):
            if prep_jobs:
                fin_ = prep_jobs.pop(0)()
                if fin_:
                    prep_deferred.append(fin_)
        if p + 1 < NP:
            emit_mtrans(p + 1)
        emit_epi_start(p)
    emit_epi_tr(NP - 1)
    emit_epi_fin(NP - 1)
    if len(TQORDER) >= 2:
        emit_spm_one(len(TQORDER) - 2, 3)
    for c in range(4):
        emit_spm_one(len(TQORDER) - 1, c)
    assert rs_emitted == set(range(NCH))
    for cb_ in prep_deferred:
        cb_()
    prep_deferred.clear()
    while prep_jobs:
        fin_ = prep_jobs.pop(0)()
        if fin_:
            fin_()
    V(lambda e: e.tensor_scalar(s_al1[:, :], s_al1[:, :], 1.0 / D, None, op0=ALU.mult),
      [f"pj_{pj[0]-1}"] + [f"pa2_{ti}_{3}" for ti in range(8)], "al1s")
    V(lambda e: e.tensor_scalar(s_al2[:, 0:8], s_al2[:, 0:8], 1.0 / FFH, None, op0=ALU.mult),
      ["al1s"], "al2s")
    prep_done = ["al1s", "al2s"] + [f"ps_{sj}" for sj in range(sj_ctr[0])]
    sp_done = [f"sps_{t}_{c}" for t in range(NQ) for c in range(4)]

    tap_specs = [("KA0", KA[0][:, :], [65, seq], BF16), ("QA0", QA[0][:, :], [65, seq], BF16),
                 ("VA", VA[:, :], [128, NH * T * 65], BF16), ("B70N", B70N[:, :], [128, NH * T], BF16)]
    tap_ms = []
    for name, ap, shape, dt in tap_specs:
        if name in taps:
            dst = nc.dram_tensor("tap_" + name, shape, dt, kind="ExternalOutput").ap()
            ST(dst, ap, f"tap_{name}", waits=list(sp_done) + [f"ngc_{NP-1}_{3}"])
            tap_ms.append(f"tap_{name}")

    es2.close()
    es_attn.close()

    es3 = contextlib.ExitStack()
    sb3 = lambda n, s, d: es3.enter_context(nc.sbuf_tensor(n, s, d))
    ps3 = lambda n, s, d: es3.enter_context(nc.psum_tensor(n, s, d))
    s_s1c = [sb3(f"s_s1c{i}", [128, 8 * 256], BF16) for i in range(4)]
    s_s2c = [sb3(f"s_s2c{i}", [128, 32 * 256], BF16) for i in range(4)]
    s_xg = [sb3(f"s_xg{i}", [128, D], F32) for i in range(4)]
    s_xn = [sb3(f"s_xn{i}", [128, D], BF16) for i in range(4)]
    xnT2 = [sb3(f"xnT{i}", [128, 8 * GSZ], BF16) for i in range(2)]
    s_actb2 = [sb3(f"s_actb{i}", [128, 32 * GSZ], BF16) for i in range(2)]
    s_gf = [sb3(f"s_gf{i}", [128, GSZ], F32) for i in range(2)]
    s_stm = [sb3(f"s_stm{i}", [128, GT * 128], F32) for i in range(2)]
    s_xtm = [sb3(f"s_xtm{i}", [128, GT * 128], F32) for i in range(2)]
    s_gtm = [sb3(f"s_gtm{i}", [128, GT * 128], F32) for i in range(2)]
    s_otm = [sb3(f"s_otm{i}", [128, GT * 128], F32) for i in range(2)]
    p_mm = [ps3(f"p_mm{i}", [128, 512], F32) for i in range(4)]
    p_t3 = [ps3(f"p_t3{i}", [128, 1024], BF16) for i in range(2)]
    p_g3 = [ps3(f"p_g3{i}", [128, 512], F32) for i in range(2)]

    barrier3 = list(sp_done) + list(tap_ms) + prep_done

    def ln3(g):
        pref = f"l3_{g}_"
        pprev = f"l3_{g-1}_"
        lw = lambda it, g=g: ([f"{pprev}a{it}"] if g >= 1 else (list(barrier3) if it == 0 else []))
        aw = lambda it, g=g: ([f"xt_{(g-1)*GT+it}_7"] if g >= 1 else [])
        ln_group(pref, GT, lambda it: s_xg[it], lambda it: s_xn[it],
                 lambda it, g=g: x_own[((OG - 1 - g) * GT + it) * 128:((OG - 1 - g) * GT + it + 1) * 128, :],
                 lw, aw, False, 48 * (g % 2),
                 extra_first=([f"l3_{g-2}_a{it}" for it in range(GT)] if g >= 2 else []),
                 jk_war=[f"{pprev}sq{GT-1}"] if g >= 1 else [])

    ln3(0)
    for g in range(OG):
        if g + 1 < OG:
            pass
        for it in range(GT):
            t = g * GT + it
            for dc in range(8):
                w = [f"l3_{g}_a{it}", "c_idb"]
                if t >= 2 and dc == 0:
                    w.append(f"xte_{t-2}_{1}")
                if dc == 4:
                    w.append(f"xte_{t}_{0}")
                PE(lambda e, t=t, it=it, dc=dc: e.transpose(
                    p_t3[t % 2][:, (dc % 4) * 128:(dc % 4) * 128 + 128],
                    s_xn[it % 4][:, dc * 128:(dc + 1) * 128], s_idb[:, :]), w, f"xt_{t}_{dc}")
                if dc == 3 or dc == 7:
                    half = dc // 4
                    wv = [f"xt_{t}_{half*4+j}" for j in range(4)]
                    if g >= 2 and it == 0 and half == 0:
                        wv.append(f"mm1_{g-2}_{31}")
                    V(lambda e, t=t, it=it, half=half, g=g: e.tensor_copy(
                        xnT2[g % 2][:, :].rearrange("p (dd c) -> p dd c", dd=8)[:, half * 4:(half + 1) * 4,
                                                                               it * 128:(it + 1) * 128],
                        p_t3[t % 2][:, 0:512].rearrange("p (k c) -> p k c", c=128)), wv, f"xte_{t}_{half}")
        xnT_ready = [f"xte_{g*GT+it}_{h}" for it in range(GT) for h in range(2)]
        if g + 1 < OG:
            ln3(g + 1)
        for hcb in range(16):
            if hcb >= 4:
                wl = [f"gel_{g}_{2*(hcb-4)+1}"]
            elif g >= 1:
                wl = [f"gel_{g-1}_{2*(hcb+12)+1}"]
            else:
                wl = list(barrier3)
            ldpool.issue(s_s1c[hcb % 4][:, :].rearrange("p (dt c) -> p dt c", c=256),
                         sg1d[:, hcb * 256:(hcb + 1) * 256].rearrange("(dt p) c -> p dt c", p=128),
                         f"s1l_{g}_{hcb}", wl)
            for hc2 in range(2):
                hc = hcb * 2 + hc2
                pb = p_mm[hc2]
                for dc in range(8):
                    w = []
                    if dc == 0:
                        w = list(xnT_ready) + [f"s1l_{g}_{hcb}"]
                        if hc >= 2:
                            w.append(f"gel_{g}_{hc-2}")
                        elif g >= 1:
                            w.append(f"gel_{g-1}_{hc+30}")
                    PE(lambda e, hcb=hcb, hc2=hc2, dc=dc, pb=pb, g=g: e.matmul(
                        pb[:, 0:GSZ], s_s1c[hcb % 4][:, dc * 256 + hc2 * 128: dc * 256 + hc2 * 128 + 128],
                        xnT2[g % 2][:, :].rearrange("p (dd c) -> p dd c", dd=8)[:, dc, 0:GSZ],
                        start=(dc == 0), stop=(dc == 7)),
                        w, f"mm1_{g}_{hc}" if dc == 7 else None)
                w = [f"mm1_{g}_{hc}", "c_b1fm", "al1s"]
                if g >= 2 and hc == 0:
                    w.append(f"mm2_{g-2}_{7}")
                A_(lambda e, g=g, hc=hc, hc2=hc2: e.activation(
                    s_actb2[g % 2][:, hc * GSZ:(hc + 1) * GSZ], p_mm[hc2][:, 0:GSZ],
                    AF.Gelu, bias=s_b1fm[:, hc:hc + 1], scale=s_al1[:, hc:hc + 1]), w, f"gel_{g}_{hc}")
        if g <= 1:
            for jc_, wj_ in [x for x in rs_pending if x[0] == (1 - g if NCH >= 4 else 0)]:
                P.op("gpsimd", lambda e, jc=jc_: e.collective_compute(
                    "ReduceScatter", ALU.add, replica_groups=[[0, 1, 2, 3], [4, 5, 6, 7]],
                    ins=[spart[jc * 4 * SW:(jc + 1) * 4 * SW, :].opt()],
                    outs=[sown[jc * SW:(jc + 1) * SW, :].opt()]),
                    waits=wj_ + [f"mm1_{g}_{15}"], inc=s_cc, inck=1, ms=f"rs{jc_}_done")
            rs_pending[:] = [x for x in rs_pending if x[0] != (1 - g if NCH >= 4 else 0)]
        for dcb in range(4):
            wl = [f"sil_{g-1}_{2*dcb+1}"] if g >= 1 else [f"gel_{g}_{0}"]
            ldpool.issue(s_s2c[dcb % 4][:, :].rearrange("p (ht c) -> p ht c", c=256),
                         sg2d[:, dcb * 256:(dcb + 1) * 256].rearrange("(ht p) c -> p ht c", p=128),
                         f"s2l_{g}_{dcb}", wl)
            for dc2 in range(2):
                dc = dcb * 2 + dc2
                pf = p_mm[2 + dc2]
                for hc in range(32):
                    w = []
                    if hc == 0:
                        w = [f"gel_{g}_{hh}" for hh in range(32)] + [f"s2l_{g}_{dcb}"]
                        if dc >= 2:
                            w.append(f"sil_{g}_{dc-2}")
                    PE(lambda e, dcb=dcb, dc2=dc2, hc=hc, pf=pf, g=g: e.matmul(
                        pf[:, 0:GSZ], s_s2c[dcb % 4][:, hc * 256 + dc2 * 128: hc * 256 + dc2 * 128 + 128],
                        s_actb2[g % 2][:, hc * GSZ:(hc + 1) * GSZ],
                        start=(hc == 0), stop=(hc == 31)), w, f"mm2_{g}_{dc}" if hc == 31 else None)
                w = [f"mm2_{g}_{dc}", "c_b2fm", "al2s"]
                if dc >= 2:
                    w.append(f"gtr_{g}_{dc-2}_{GT-1}")
                elif g >= 1:
                    w.append(f"gtr_{g-1}_{dc+6}_{GT-1}")
                A_(lambda e, g=g, dc=dc, dc2=dc2: e.activation(
                    s_gf[dc % 2][:, :], p_mm[2 + dc2][:, 0:GSZ], AF.Silu,
                    bias=s_b2fm[:, dc:dc + 1], scale=s_al2[:, dc:dc + 1]), w, f"sil_{g}_{dc}")
                for st in range(GT):
                    w = [f"sil_{g}_{dc}"] if st == 0 else []
                    if st == 0 and (dc >= 2 or g >= 1):
                        w.append(f"gte_{g}_{dc-2}" if dc >= 2 else f"gte_{g-1}_{dc+6}")
                    PE(lambda e, dc=dc, st=st: e.transpose(
                        p_g3[dc % 2][:, st * 128:(st + 1) * 128], s_gf[dc % 2][:, st * 128:(st + 1) * 128],
                        s_id[:, :]), w, f"gtr_{g}_{dc}_{st}")
                w = [f"gtr_{g}_{dc}_{GT-1}"]
                if dc >= 2:
                    w.append(f"cm1_{g}_{dc-2}")
                elif g >= 1:
                    w.append(f"cm1_{g-1}_{dc+6}")
                V(lambda e, g=g, dc=dc: e.tensor_copy(s_gtm[dc % 2][:, :], p_g3[dc % 2][:, 0:GT * 128]), w,
                  f"gte_{g}_{dc}")
                gm = OG - 1 - g
                SWm = max(seq // 16, 128)
                w = [f"rs{j}_done" for j in range((gm * GSZ) // SWm, (gm * GSZ + GSZ - 1) // SWm + 1)]
                if dc >= 2:
                    w.append(f"cm1_{g}_{dc-2}")
                elif g >= 1:
                    w.append(f"cm1_{g-1}_{dc+6}")
                stpool.issue(s_stm[dc % 2][:, :].rearrange("p (s c) -> p s c", c=128),
                             sown[gm * GSZ:(gm + 1) * GSZ, dc * 128:(dc + 1) * 128].rearrange("(s p) c -> p s c", p=128),
                             f"sld_{g}_{dc}", w)
                w2_ = [f"cmb_{g}_{dc-2}"] if dc >= 2 else ([f"cmb_{g-1}_{dc+6}"] if g >= 1 else [])
                stpool.issue(s_xtm[dc % 2][:, :].rearrange("p (s c) -> p s c", c=128),
                             x_own[gm * GSZ:(gm + 1) * GSZ, dc * 128:(dc + 1) * 128].rearrange("(s p) c -> p s c", p=128),
                             f"xld_{g}_{dc}", w2_)
                w = [f"gte_{g}_{dc}", f"sld_{g}_{dc}"]
                if dc >= 2:
                    w.append(f"yst_{g}_{dc-2}")
                elif g >= 1:
                    w.append(f"yst_{g-1}_{dc+6}")
                V(lambda e, dc=dc: e.tensor_tensor(s_otm[dc % 2][:, :], s_stm[dc % 2][:, :],
                                                   s_gtm[dc % 2][:, :], op=ALU.mult), w, f"cm1_{g}_{dc}")
                V(lambda e, dc=dc: e.tensor_tensor(s_otm[dc % 2][:, :], s_otm[dc % 2][:, :],
                                                   s_xtm[dc % 2][:, :], op=ALU.add),
                  [f"cm1_{g}_{dc}", f"xld_{g}_{dc}"], f"cmb_{g}_{dc}")
                ST(y[gm * GSZ:(gm + 1) * GSZ, dc * 128:(dc + 1) * 128].rearrange("(s p) c -> p s c", p=128),
                   s_otm[dc % 2][:, :].rearrange("p (s c) -> p s c", c=128),
                   f"yst_{g}_{dc}", waits=[f"cmb_{g}_{dc}"])

    y_done = [f"yst_{g}_{dc}" for g in range(OG) for dc in range(8)]

    final_waits = y_done + [(s, s.n) for s in stpool.sems]
    P.op("sync", lambda e: e.nop(), final_waits, None)

    es3.close()

    with nc.Block() as block:
        @block.sync
        def _(eng):
            P.emit_engine("sync", eng)

        @block.vector
        def _(eng):
            P.emit_engine("vector", eng)

        @block.scalar
        def _(eng):
            P.emit_engine("scalar", eng)

        @block.tensor
        def _(eng):
            P.emit_engine("tensor", eng)

        @block.gpsimd
        def _(eng):
            P.emit_engine("gpsimd", eng)

    es.close()
    return nc


def host_inputs(inputs, seq):
    x = np.ascontiguousarray(np.asarray(inputs["L0_attn_x"], dtype=np.float32))
    wqkv_full = np.asarray(inputs["L0_attn_w_qkv"], dtype=np.float32)
    wout = np.asarray(inputs["L0_attn_w_out"], dtype=np.float32)
    OWN = seq // 4
    ident = np.eye(128, dtype=np.float32)
    qi, ki = np.arange(128)[:, None], np.arange(128)[None, :]
    maskxneg = np.where(ki > qi, np.float32(NEG_BIG), np.float32(0.0))
    mask01 = np.where(np.arange(128)[:, None] <= np.arange(128)[None, :],
                      np.float32(1.0), np.float32(0.0))
    shared = dict(
        w1=np.ascontiguousarray(np.asarray(inputs["L0_ffn_w1"], dtype=np.float32)),
        w2=np.ascontiguousarray(np.asarray(inputs["L0_ffn_w2"], dtype=np.float32)),
        w1t=np.ascontiguousarray(np.asarray(inputs["L0_ffn_w1"], dtype=np.float32).T),
        w2t=np.ascontiguousarray(np.asarray(inputs["L0_ffn_w2"], dtype=np.float32).T),
        gamma=np.ascontiguousarray(np.asarray(inputs["L0_attn_ln_gamma"], dtype=np.float32)),
        beta=np.ascontiguousarray(np.asarray(inputs["L0_attn_ln_beta"], dtype=np.float32)),
        b_out=np.ascontiguousarray(np.asarray(inputs["L0_attn_b_out"], dtype=np.float32)),
        b1=np.ascontiguousarray(np.asarray(inputs["L0_ffn_b1"], dtype=np.float32)),
        b2=np.ascontiguousarray(np.asarray(inputs["L0_ffn_b2"], dtype=np.float32)),
        ident=ident, maskxneg=maskxneg, mask01=mask01,
    )
    in_maps = []
    for c in range(8):
        b, hg = c // 4, c % 4
        sl = slice(256 * hg, 256 * hg + 256)
        wq = np.ascontiguousarray(np.concatenate(
            [wqkv_full[0:1024][sl], wqkv_full[1024:2048][sl], wqkv_full[2048:3072][sl]], axis=0))
        in_maps.append(dict(
            x_seq=np.ascontiguousarray(x[b]),
            x_own=np.ascontiguousarray(x[b, hg * OWN:(hg + 1) * OWN]),
            wqkv=wq,
            woutT=np.ascontiguousarray(wout[:, sl].T),
            **shared,
        ))
    return in_maps


def kernel(**inputs):
    seq = int(np.asarray(inputs["L0_attn_x"]).shape[1])
    nc = build(seq=seq)
    in_maps = host_inputs(inputs, seq)
    res = run_bass_kernel_spmd(nc, in_maps, core_ids=list(range(8)))
    OWN = seq // 4
    out = np.empty((2, seq, D), dtype=np.float32)
    for c in range(8):
        b, hg = c // 4, c % 4
        out[b, hg * OWN:(hg + 1) * OWN] = res.results[c]["y"]
    return out


# version 31
# speedup vs baseline: 1.0232x; 1.0232x over previous
import contextlib
import numpy as np
import concourse.bass as bass
import concourse.mybir as mybir
from concourse.bass_utils import run_bass_kernel_spmd

F32 = mybir.dt.float32
BF16 = mybir.dt.bfloat16
AF = mybir.ActivationFunctionType
ALU = mybir.AluOpType
AX = mybir.AxisListType

D = 1024
NH = 4
HD = 256
FFH = 4096
NEG_BIG = -3.0e38
SHIFT = 70.0
EPS = 1e-5
ALU_MIN = getattr(ALU, "min")


class Sem:
    __slots__ = ("h", "name", "n")

    def __init__(self, h, name):
        self.h, self.name, self.n = h, name, 0


class Plan:
    def __init__(self):
        self.ops = []
        self.record = {}

    def op(self, eng, emit, waits=(), inc=None, inck=1, ms=None):
        for w in waits:
            if isinstance(w, str) and w not in self.record:
                raise KeyError(f"undefined milestone {w!r} ({eng})")
        self.ops.append(dict(eng=eng, emit=emit, waits=list(waits), inc=inc, inck=inck))
        if inc is not None:
            inc.n += inck
            if ms is not None:
                self.record[ms] = (inc, inc.n)
        else:
            assert ms is None

    def emit_engine(self, engname, eng):
        hw = {}
        for o in self.ops:
            if o["eng"] != engname:
                continue
            for w in o["waits"]:
                sem, val = self.record[w] if isinstance(w, str) else w
                if hw.get(sem.name, -1) < val:
                    eng.wait_ge(sem.h, val)
                    hw[sem.name] = val
            inst = o["emit"](eng)
            if o["inc"] is not None:
                inst.then_inc(o["inc"].h, o["inck"])


def build(seq=8192, taps=()):
    T = seq // 128
    NQ = seq // 512
    OWN = seq // 4
    GSZ = min(512, OWN)
    OG = OWN // GSZ
    GT = GSZ // 128
    assert seq % 512 == 0 and GT % 2 == 0, (seq, GT)

    nc = bass.Bass("TRN2", target_bir_lowering=False)

    din = lambda name, shape: nc.dram_tensor(name, shape, F32, kind="ExternalInput").ap()
    x_seq = din("x_seq", [seq, D])
    x_own = din("x_own", [OWN, D])
    wqkv = din("wqkv", [768, D])
    woutT = din("woutT", [HD, D])
    w1 = din("w1", [FFH, D])
    w2 = din("w2", [D, FFH])
    w1t = din("w1t", [D, FFH])
    w2t = din("w2t", [FFH, D])
    gamma = din("gamma", [D])
    beta = din("beta", [D])
    b_out = din("b_out", [D])
    b1 = din("b1", [FFH])
    b2 = din("b2", [D])
    ident = din("ident", [128, 128])
    maskxneg = din("maskxneg", [128, 128])
    mask01 = din("mask01", [128, 128])
    y = nc.dram_tensor("y", [OWN, D], F32, kind="ExternalOutput").ap()

    def dscr(name, shape, dt):
        kind = "ExternalOutput" if name in taps else "Internal"
        return nc.dram_tensor(name, shape, dt, kind=kind).ap()

    attn_q = dscr("attn_q", [seq, HD], BF16)
    sg1d = dscr("sg1d", [D, FFH], BF16)
    sg2d = dscr("sg2d", [FFH, D], BF16)
    spart = dscr("spart", [seq, D], F32)
    sown = dscr("sown", [OWN, D], F32)

    P = Plan()
    es = contextlib.ExitStack()
    sb = lambda n, s, d: es.enter_context(nc.sbuf_tensor(n, s, d))
    mksem = lambda n: Sem(es.enter_context(nc.semaphore(n)), n)

    s_pe = mksem("s_pe")
    s_dve = mksem("s_dve")
    s_act = mksem("s_act")
    s_cc = mksem("s_cc")

    class DmaPool:
        def __init__(self, tag, n, eng):
            self.sems = [mksem(f"{tag}{i}") for i in range(n)]
            self.n = n
            self.eng = eng
            self.idx = 0

        def issue(self, dst, src, ms, waits=()):
            j = self.idx
            self.idx += 1
            sem = self.sems[j % self.n]
            gate = [(sem, 16 * (j // self.n))] if j >= self.n else []
            P.op(self.eng, lambda e, d=dst, s=src: e.dma_start(out=d, in_=s),
                 list(waits) + gate, sem, 16, ms)

    ldpool = DmaPool("ldq", 8, "sync")
    stpool = DmaPool("stq", 6, "scalar")
    gppool = DmaPool("gpq", 4, "gpsimd")

    s_id = sb("s_id", [128, 128], F32)
    s_idb = sb("s_idb", [128, 128], BF16)
    s_onesb = sb("s_onesb", [1, 128], BF16)
    s_onescol = sb("s_onescol", [64, 1], BF16)
    s_b1fm = sb("s_b1fm", [128, 32], F32)
    s_b2fm = sb("s_b2fm", [128, 8], F32)
    s_stat = sb("s_stat", [128, 96], F32)
    s_al1 = sb("s_al1", [128, 32], F32)
    s_al2 = sb("s_al2", [128, 12], F32)
    s_eps = sb("s_eps", [128, 1], F32)
    es_attn = contextlib.ExitStack()
    sba = lambda n, s, d: es_attn.enter_context(nc.sbuf_tensor(n, s, d))
    KA = [sba(f"KA{h}", [65, seq], BF16) for h in range(NH)]
    QA = [sba(f"QA{h}", [65, seq], BF16) for h in range(NH)]
    VA = sba("VA", [128, NH * T * 65], BF16)
    B70N = sba("B70N", [128, NH * T], BF16)

    V = lambda emit, waits, ms=None: P.op("vector", emit, waits, s_dve, 1, ms)
    A_ = lambda emit, waits, ms=None: P.op("scalar", emit, waits, s_act, 1, ms)
    PE = lambda emit, waits, ms=None: P.op("tensor", emit, waits, s_pe, 1, ms)
    LD = lambda dst, src, ms, waits=(): ldpool.issue(dst, src, ms, waits)
    ST = lambda dst, src, ms, waits=(): stpool.issue(dst, src, ms, waits)

    es0 = contextlib.ExitStack()
    sb0 = lambda n, s, d: es0.enter_context(nc.sbuf_tensor(n, s, d))
    esp0 = contextlib.ExitStack()
    ps0 = lambda n, s, d: esp0.enter_context(nc.psum_tensor(n, s, d))

    WqkvT = sb0("WqkvT", [128, 48 * 128], BF16)
    s_bWfm = sb0("s_bWfm", [128, 6], F32)
    s_bWvA = sb0("s_bWvA", [1, 128], BF16)
    s_bWvB = sb0("s_bWvB", [1, 128], BF16)
    s_x = [sb0(f"s_x{i}", [128, D], F32) for i in range(4)]
    s_a = [sb0(f"s_a{i}", [128, D], BF16) for i in range(4)]
    es0t = contextlib.ExitStack()
    sb0t = lambda n, s, d: es0t.enter_context(nc.sbuf_tensor(n, s, d))
    s_wqt = [sb0t(f"s_wqt{i}", [128, 128], F32) for i in range(2)]
    s_wqtb = [sb0t(f"s_wqtb{i}", [128, 128], BF16) for i in range(2)]
    s_g8 = sb0t("s_g8", [8, 128], F32)
    s_b8 = sb0t("s_b8", [8, 128], F32)
    s_b1n = sb0t("s_b1n", [32, 128], F32)
    s_b2n = sb0t("s_b2n", [8, 128], F32)
    s_gfm = sb0t("s_gfm", [128, 8], F32)
    s_bfm = sb0t("s_bfm", [128, 8], F32)
    s_bfmb = sb0t("s_bfmb", [128, 8], BF16)
    s_m01f = sb0t("s_m01f", [128, 128], F32)
    p0m = ps0("p0m", [128, 512], F32)
    p0b = [ps0(f"p0b{i}", [128, 1024], BF16) for i in range(2)]
    p0w = ps0("p0w", [128, 16], F32)

    LD(s_id[:, :], ident, "ld_id")
    LD(s_g8[:, :], gamma.rearrange("(c p) -> c p", p=128), "ld_g")
    LD(s_b8[:, :], beta.rearrange("(c p) -> c p", p=128), "ld_b")
    LD(s_b1n[:, :], b1.rearrange("(c p) -> c p", p=128), "ld_b1")
    LD(s_b2n[:, :], b2.rearrange("(c p) -> c p", p=128), "ld_b2")

    V(lambda e: e.tensor_copy(s_idb[:, :], s_id[:, :]), ["ld_id"], "c_idb")
    V(lambda e: e.memset(s_onesb[:, :], 1.0), [], "c_ones")
    V(lambda e: e.memset(s_eps[:, :], float(EPS)), [], "c_eps")
    V(lambda e: e.memset(s_onescol[:, :], 1.0), [], "c_onescol")
    for h in range(NH):
        V(lambda e, h=h: e.memset(KA[h][64:65, :], 1.0), [], f"cka_{h}")
    V(lambda e: e.memset(VA[:, :].rearrange("p (s c) -> p s c", c=65)[:, :, 64:65], 1.0), [], "c_va1")

    PE(lambda e: e.transpose(p0m[:, 0:8], s_g8[:, :], s_id[0:8, 0:8]), ["ld_g", "ld_id"], "t_g")
    V(lambda e: e.tensor_copy(s_gfm[:, :], p0m[:, 0:8]), ["t_g"], "c_gfm")
    PE(lambda e: e.transpose(p0m[:, 8:16], s_b8[:, :], s_id[0:8, 0:8]), ["ld_b", "c_gfm"], "t_b")
    V(lambda e: e.tensor_copy(s_bfm[:, :], p0m[:, 8:16]), ["t_b"], "c_bfm")
    V(lambda e: e.tensor_copy(s_bfmb[:, :], s_bfm[:, :]), ["c_bfm"], "c_bfmb")
    PE(lambda e: e.transpose(p0m[:, 16:48], s_b1n[:, :], s_id[0:32, 0:32]), ["ld_b1", "c_bfm"], "t_b1")
    V(lambda e: e.tensor_copy(s_b1fm[:, :], p0m[:, 16:48]), ["t_b1"], "c_b1fm")
    PE(lambda e: e.transpose(p0m[:, 48:56], s_b2n[:, :], s_id[0:8, 0:8]), ["ld_b2", "c_b1fm"], "t_b2")
    V(lambda e: e.tensor_copy(s_b2fm[:, :], p0m[:, 48:56]), ["t_b2"], "c_b2fm")

    wq_chunk = lambda fc, dc: WqkvT[:, (fc * 8 + dc) * 128:(fc * 8 + dc) * 128 + 128]
    tiles06 = [(fc, dc) for fc in range(6) for dc in range(8)]
    for ki in range(48):
        fc, dc = tiles06[ki]
        gi = ki // 4
        j = ki % 4
        buf = p0b[gi % 2]
        w = [f"cwq_{ki-2}"] if ki >= 2 else []
        ldpool.issue(s_wqt[ki % 2][:, :], wqkv[fc * 128:(fc + 1) * 128, dc * 128:(dc + 1) * 128],
                     f"lwq_{ki}", w)
        V(lambda e, ki=ki: e.tensor_copy(s_wqtb[ki % 2][:, :], s_wqt[ki % 2][:, :]),
          [f"lwq_{ki}"] + ([f"t0_{(ki-2)//4}_{(ki-2)%4}"] if ki >= 2 else []), f"cwq_{ki}")
        PE(lambda e, ki=ki, gi=gi, j=j, buf=buf: e.transpose(
            buf[:, j * 128:(j + 1) * 128], s_wqtb[ki % 2][:, :], s_idb[:, :]),
            [f"cwq_{ki}", "c_idb"] + ([f"e0_{gi-2}"] if (j == 0 and gi >= 2) else []), f"t0_{gi}_{j}")
        if j == 3:
            V(lambda e, gi=gi, buf=buf: e.tensor_copy(
                WqkvT[:, :].rearrange("p (k c) -> p k c", c=128)[:, gi * 4:(gi + 1) * 4, :],
                buf[:, 0:512].rearrange("p (k c) -> p k c", c=128)),
                [f"t0_{gi}_{jj}" for jj in range(4)], f"e0_{gi}")
    e0_of = lambda fc, dc: f"e0_{(fc * 8 + dc) // 4}"

    for fc in range(6):
        for dc in range(8):
            PE(lambda e, fc=fc, dc=dc: e.matmul(p0w[:, fc:fc + 1], wq_chunk(fc, dc), s_bfmb[:, dc:dc + 1],
                                                start=(dc == 0), stop=(dc == 7)),
               [e0_of(fc, dc), "c_bfmb"], f"mmbw_{fc}" if dc == 7 else None)
    allbw = [f"mmbw_{fc}" for fc in range(6)]
    for fc in range(6):
        sc = 0.125 if fc < 2 else 1.0
        V(lambda e, fc=fc, sc=sc: e.tensor_scalar(s_bWfm[:, fc:fc + 1], p0w[:, fc:fc + 1], sc, None, op0=ALU.mult),
          allbw, f"cbw_{fc}")
    for i in range(2):
        PE(lambda e, i=i: e.transpose(p0m[0:1, 64 + i * 128:64 + (i + 1) * 128], s_bWfm[:, 4 + i:5 + i], s_id[:, :]),
           [f"cbw_{j}" for j in range(6)] + ["c_b2fm"], f"tbwv_{i}")
        V(lambda e, i=i: e.tensor_copy((s_bWvA if i == 0 else s_bWvB)[:, :],
                                       p0m[0:1, 64 + i * 128:64 + (i + 1) * 128]),
          [f"tbwv_{i}"], f"cbwv_{i}")
    for fc in range(6):
        for dc in range(8):
            if fc < 2:
                V(lambda e, fc=fc, dc=dc: e.tensor_scalar(wq_chunk(fc, dc), wq_chunk(fc, dc),
                                                          s_gfm[:, dc:dc + 1], 0.125, op0=ALU.mult, op1=ALU.mult),
                  ["c_gfm", "cbwv_0", "cbwv_1"] + allbw, f"scwq_{fc}_{dc}")
            else:
                V(lambda e, fc=fc, dc=dc: e.tensor_scalar(wq_chunk(fc, dc), wq_chunk(fc, dc),
                                                          s_gfm[:, dc:dc + 1], None, op0=ALU.mult),
                  ["c_gfm", "cbwv_0", "cbwv_1"] + allbw, f"scwq_{fc}_{dc}")
    wq_ready = [f"scwq_{fc}_{dc}" for fc in range(6) for dc in range(8)]

    esp0.close()
    es0t.close()
    ph0_done = ["scwq_5_7", "tbwv_1", "cbwv_1", "c_b1fm", "c_b2fm"]

    esp1 = contextlib.ExitStack()
    ps1 = lambda n, s, d: esp1.enter_context(nc.psum_tensor(n, s, d))
    s_aT = sb0("s_aT", [128, 4 * D], BF16)
    p_tr = [ps1(f"p_tr{i}", [128, 1024], BF16) for i in range(2)]
    p_qk = [ps1(f"p_qk{i}", [128, 1024], F32) for i in range(2)]
    p_v = [ps1(f"p_v{i}", [128, 512], F32) for i in range(2)]

    def ln_group(pref, ntile, xbuf, abuf, xsrc, load_waits, abuf_waits, two_std, off, extra_first=(), jk_war=()):
        red = []
        for it in range(ntile):
            w = list(load_waits(it))
            if it >= 2:
                w = [f"{pref}sum{it-2}", f"{pref}sq{it-2}"]
            ldpool.issue(xbuf(it)[:, :], xsrc(it), f"{pref}x{it}", w)
            V(lambda e, it=it: e.reduce_sum(s_stat[:, off + it:off + it + 1], xbuf(it)[:, :], axis=AX.X),
              [f"{pref}x{it}"] + (list(extra_first) if it == 0 else []), f"{pref}sum{it}")
            A_(lambda e, it=it: e.activation(
                abuf(it)[:, :], xbuf(it)[:, :], AF.Square, bias=0.0, scale=1.0,
                accum_out=s_stat[:, off + 4 + it:off + 5 + it]),
                [f"{pref}x{it}"] + list(abuf_waits(it)), f"{pref}sq{it}")
            red += [f"{pref}sum{it}", f"{pref}sq{it}"]
        n = ntile
        sl = lambda c: s_stat[:, off + c:off + c + n]
        V(lambda e: e.tensor_scalar(sl(8), sl(0), 1.0 / D, None, op0=ALU.mult),
          red + list(extra_first), pref + "m")
        V(lambda e: e.tensor_scalar(sl(16), sl(4), 1.0 / D, None, op0=ALU.mult), red, pref + "e2")
        V(lambda e: e.tensor_tensor(sl(12), sl(8), sl(8), op=ALU.mult), [pref + "m"], pref + "msq")
        V(lambda e: e.tensor_tensor(sl(16), sl(16), sl(12), op=ALU.subtract),
          [pref + "e2", pref + "msq"], pref + "v")
        A_(lambda e: e.activation(sl(20), sl(16), AF.Sqrt, bias=s_eps[:, :], scale=1.0), [pref + "v", "c_eps"], pref + "s1")
        V(lambda e: e.reciprocal(sl(24), sl(20)), [pref + "s1"], pref + "rstd")
        if two_std:
            V(lambda e: e.tensor_tensor(sl(28), sl(24), sl(24), op=ALU.mult), [pref + "rstd"], pref + "r2")
            V(lambda e: e.tensor_tensor(sl(28), sl(28), sl(16), op=ALU.mult), [pref + "r2", pref + "v"], pref + "v2")
            A_(lambda e: e.activation(sl(32), sl(28), AF.Sqrt, bias=s_eps[:, :], scale=1.0), [pref + "v2"], pref + "s2")
            V(lambda e: e.reciprocal(sl(36), sl(32)), [pref + "s2"], pref + "rstd2")
            V(lambda e: e.tensor_tensor(sl(40), sl(24), sl(36), op=ALU.mult),
              [pref + "rstd2", pref + "rstd"], pref + "sc")
            scc = 40
        else:
            scc = 24
        for it in range(ntile):
            V(lambda e, it=it: e.tensor_scalar(
                abuf(it)[:, :], xbuf(it)[:, :], s_stat[:, off + 8 + it:off + 9 + it],
                s_stat[:, off + scc + it:off + scc + it + 1], op0=ALU.subtract, op1=ALU.mult),
                [pref + ("sc" if two_std else "rstd")] + list(abuf_waits(it)), f"{pref}a{it}")

    NG1 = T // 4

    def ln1(g4):
        pref = f"l1_{g4}_"
        pprev = f"l1_{g4-1}_"
        lw = lambda it, g4=g4: ([f"{pprev}a{it}"] if g4 >= 1 else [])
        aw = lambda it, g4=g4: ([f"tr_{(g4-1)*4+it}_7"] if g4 >= 1 else [])
        ln_group(pref, 4, lambda it: s_x[it], lambda it: s_a[it],
                 lambda it, g4=g4: x_seq[(g4 * 4 + it) * 128:(g4 * 4 + it + 1) * 128, :],
                 lw, aw, True, 48 * (g4 % 2),
                 extra_first=([f"l1_{g4-2}_a{it}" for it in range(4)] if g4 >= 2 else []),
                 jk_war=[f"{pprev}sq{3}"] if g4 >= 1 else [])

    ln1(0)
    for g4 in range(NG1):
        for it in range(4):
            t = g4 * 4 + it
            for dc in range(8):
                w = [f"l1_{g4}_a{it}", "c_idb"]
                if t == 0 and dc == 0:
                    w += ["cbwv_0", "cbwv_1", "e0_11"]
                if t >= 2:
                    w.append(f"aTe_{t-2}_1")
                if dc == 4:
                    w.append(f"aTe_{t}_0")
                PE(lambda e, t=t, it=it, dc=dc: e.transpose(
                    p_tr[t % 2][:, (dc % 4) * 128:(dc % 4) * 128 + 128],
                    s_a[it % 4][:, dc * 128:(dc + 1) * 128], s_idb[:, :]), w, f"tr_{t}_{dc}")
                if dc == 3 or dc == 7:
                    half = dc // 4
                    wv = [f"tr_{t}_{half*4+j}" for j in range(4)]
                    if g4 >= 1 and it == 0 and half == 0:
                        wv += [f"qk_{g4-1}_{fc}" for fc in range(4)] + [f"vm_{(g4-1)*4+j}" for j in range(4)]
                    V(lambda e, t=t, it=it, half=half: e.tensor_copy(
                        s_aT[:, it * D + half * 512: it * D + half * 512 + 512],
                        p_tr[t % 2][:, 0:512]), wv, f"aTe_{t}_{half}")
        if g4 + 1 < NG1:
            ln1(g4 + 1)
        aT = s_aT
        aTready = [f"aTe_{g4*4+it}_{h}" for it in range(4) for h in range(2)]
        for half in range(2):
            pq = p_qk[half]
            for fci in range(2):
                fc = half * 2 + fci
                for dc in range(8):
                    w = []
                    if dc == 0:
                        w = list(aTready)
                        if g4 == 0:
                            w += wq_ready
                        if g4 >= 1:
                            w += [f"qke_{g4-1}_{fc}_{s}" for s in range(2)]
                    PE(lambda e, fc=fc, dc=dc, fci=fci, pq=pq, aT=aT: e.matmul(
                        pq[:, fci * 512:fci * 512 + 512], wq_chunk(fc, dc),
                        aT[:, :].rearrange("p (t d) -> p t d", d=D)[:, :, dc * 128:dc * 128 + 128],
                        start=(dc == 0), stop=(dc == 7)),
                        w, f"qk_{g4}_{fc}" if dc == 7 else None)
            for fci in range(2):
                fc = half * 2 + fci
                for sub in range(2):
                    hh = (fc % 2) * 2 + sub
                    dst = (QA if fc < 2 else KA)[hh]
                    A_(lambda e, dst=dst, sub=sub, fci=fci, pq=pq, g4=g4, fc=fc: e.activation(
                        dst[0:64, g4 * 512:(g4 + 1) * 512],
                        pq[sub * 64:sub * 64 + 64, fci * 512:fci * 512 + 512],
                        AF.Identity, bias=s_bWfm[sub * 64:sub * 64 + 64, fc:fc + 1], scale=1.0),
                        [f"qk_{g4}_{fc}", f"cbw_{fc}"], f"qke_{g4}_{fc}_{sub}")
        for it in range(4):
            t = g4 * 4 + it
            pv = p_v[t % 2]
            for dc in range(8):
                w = []
                if dc == 0:
                    w = list(aTready) + ["cbwv_0", "cbwv_1"]
                    if t >= 2:
                        w += [f"ve_{t-2}_{s}" for s in range(4)]
                PE(lambda e, it=it, dc=dc, aT=aT, pv=pv: e.matmul(
                    pv[:, 0:256], aT[:, it * D + dc * 128: it * D + dc * 128 + 128],
                    WqkvT[:, :].rearrange("p (f k) -> p f k", f=6)[:, 4:6, dc * 128:dc * 128 + 128],
                    start=(dc == 0), stop=False), w, None)
            PE(lambda e, pv=pv: e.matmul(pv[:, 0:128], s_onesb[:, :], s_bWvA[:, :], start=False, stop=False),
               [], None)
            PE(lambda e, pv=pv: e.matmul(pv[:, 128:256], s_onesb[:, :], s_bWvB[:, :], start=False, stop=True),
               [], f"vm_{t}")
            for sub in range(4):
                V(lambda e, t=t, sub=sub, pv=pv: e.tensor_copy(
                    VA[:, (sub * T + t) * 65:(sub * T + t) * 65 + 64], pv[:, sub * 64:sub * 64 + 64]),
                    [f"vm_{t}", "c_va1"], f"ve_{t}_{sub}")
    p1_done = ([f"qke_{NG1-1}_{fc}_{s}" for fc in range(4) for s in range(2)] +
               [f"ve_{T-1}_{s}" for s in range(4)] + [f"ve_{T-2}_{s}" for s in range(4)])

    s_k2 = sb0("s_k2", [128, 64], F32)
    s_sc1 = sb0("s_sc1", [1, 4], F32)
    s_sc1b = sb0("s_sc1b", [1, 4], BF16)
    s_mkb = sb0("s_mkb", [128, 4], F32)
    NCH = (T + 7) // 8
    chs = [(c * 8, min(8, T - c * 8)) for c in range(NCH)]
    prev_end = ["cbw_0"]
    prev2 = ["cbw_0"]
    for h in range(NH):
        for side in range(2):
            src = (QA if side == 0 else KA)[h]
            for ch, (c0, nt) in enumerate(chs):
                w = list(p1_done) + list(prev_end)
                A_(lambda e, src=src, c0=c0, nt=nt: e.activation(
                    s_aT[0:64, 0:nt * 128], src[0:64, c0 * 128:(c0 + nt) * 128], AF.Square,
                    bias=0.0, scale=1.0), w, f"bsq_{h}_{side}_{ch}")
                ci = (h * 2 + side) * NCH + ch
                pv = p_v[ci % 2]
                for j in range(nt):
                    PE(lambda e, j=j, pv=pv: e.matmul(pv[:, 256 + j:257 + j],
                                               s_aT[0:64, j * 128:(j + 1) * 128], s_onescol[:, :],
                                               start=True, stop=True),
                       [f"bsq_{h}_{side}_{ch}", "c_onescol"] + list(prev2), f"bmm_{h}_{side}_{ch}_{j}")
                dst = (B70N[:, h * T + c0: h * T + c0 + nt] if side == 0
                       else s_k2[:, c0:c0 + nt])
                V(lambda e, dst=dst, nt=nt, pv=pv: e.tensor_copy(dst, pv[:, 256:256 + nt]),
                  [f"bmm_{h}_{side}_{ch}_{j}" for j in range(nt)], f"b2e_{h}_{side}_{ch}")
                prev2 = list(prev_end)
                prev_end = [f"b2e_{h}_{side}_{ch}"]
        V(lambda e, h=h: e.reduce_max(s_stat[:, 91:92], s_k2[:, 0:T], axis=AX.X),
          [f"b2e_{h}_1_{ch}" for ch in range(NCH)], f"bk1_{h}")
        PE(lambda e: e.transpose(p_qk[0][0:1, 0:128], s_stat[:, 91:92], s_id[:, :]),
           [f"bk1_{h}"] + list(p1_done) + ([f"bk6_{h-1}"] if h >= 1 else []), f"bk2_{h}")
        V(lambda e, h=h: e.reduce_max(s_sc1[:, h:h + 1], p_qk[0][0:1, 0:128], axis=AX.X), [f"bk2_{h}"], f"bk3_{h}")
        A_(lambda e, h=h: e.activation(s_sc1[:, h:h + 1], s_sc1[:, h:h + 1], AF.Sqrt, bias=0.0, scale=1.0),
           [f"bk3_{h}"], f"bk4s_{h}")
        V(lambda e, h=h: e.tensor_scalar(s_sc1b[:, h:h + 1], s_sc1[:, h:h + 1], -1.0, None, op0=ALU.mult),
          [f"bk4s_{h}"], f"bk4_{h}")
        PE(lambda e, h=h: e.matmul(p_qk[0][:, 128 + h:129 + h], s_onesb[:, :], s_sc1b[:, h:h + 1],
                                   start=True, stop=True), [f"bk4_{h}", f"bk3_{h}"], f"bk5_{h}")
        V(lambda e, h=h: e.tensor_copy(s_mkb[:, h:h + 1], p_qk[0][:, 128 + h:129 + h]), [f"bk5_{h}"], f"bk6_{h}")
        A_(lambda e, h=h: e.activation(B70N[:, h * T:(h + 1) * T], B70N[:, h * T:(h + 1) * T], AF.Sqrt,
                                       bias=0.0, scale=1.0),
           [f"b2e_{h}_0_{ch}" for ch in range(NCH)], f"bq1_{h}")
        V(lambda e, h=h: e.tensor_scalar(B70N[:, h * T:(h + 1) * T], B70N[:, h * T:(h + 1) * T],
                                         s_mkb[:, h:h + 1], SHIFT, op0=ALU.mult, op1=ALU.add),
          [f"bq1_{h}", f"bk6_{h}"], f"b70_{h}")
    b_ready = [f"b70_{h}" for h in range(NH)]

    es0.close()
    esp1.close()

    es2 = contextlib.ExitStack()
    sb2 = lambda n, s, d: es2.enter_context(nc.sbuf_tensor(n, s, d))
    ps2 = lambda n, s, d: es2.enter_context(nc.psum_tensor(n, s, d))
    PT = [sb2(f"PT{i}", [128, 1536], BF16) for i in range(3)]
    s_mxn = sb2("s_mxn", [128, 128], F32)
    s_m01 = sb2("s_m01", [128, 128], BF16)
    s_dj = sb2("s_dj", [128, 128], F32)
    s_dj2 = s_dj
    s_negL = sb2("s_negL", [128, 4], F32)
    s_negm = sb2("s_negm", [128, 4], F32)
    s_oT = [sb2("s_oT0", [65, 512], F32)]
    s_rl = sb2("s_rl", [128, 4], F32)
    s_ast = [sb2(f"s_ast{i}", [128, 256], BF16) for i in range(2)]
    WoT2 = sb2("WoT2", [128, 2 * D], BF16)
    s_astF2 = [sb2(f"s_astF2{i}", [128, 1024], BF16) for i in range(2)]
    s_sp = [sb2(f"s_sp{i}", [128, D], F32) for i in range(2)]
    s_boutr = sb2("s_boutr", [1, D], BF16)
    LD(s_mxn[:, :], maskxneg, "ld_mx", waits=list(p1_done))
    LD(s_dj[:, :], mask01, "ld_m01", waits=list(p1_done))
    V(lambda e: e.tensor_copy(s_m01[:, :], s_dj[:, :]), ["ld_m01"], "c_m01")
    p_sy = [ps2(f"p_sy{i}", [128, 1536], F32) for i in range(2)]
    for i_ in range(2):
        V(lambda e, i_=i_: e.memset(p_sy[i_][:, :], 0.0), list(p1_done), f"syz_{i_}")
    p_oa = [ps2(f"p_oa{i}", [128, 512], F32) for i in range(2)]
    p_dg_of = lambda p: p_oa[p % 2]
    p_ep_of = lambda p: p_oa[p % 2]

    s_wp = [sb2(f"s_wp{i}", [128, 1024], F32) for i in range(2)]
    for j_ in range(2):
        LD(s_wp[0][:, :], woutT[j_ * 128:(j_ + 1) * 128, :], f"ld_wot{j_}",
           waits=list(p1_done) + ([f"c_wot{j_-1}"] if j_ else []))
        V(lambda e, j_=j_: e.tensor_copy(WoT2[:, j_ * D:(j_ + 1) * D], s_wp[0][:, :]),
          [f"ld_wot{j_}"], f"c_wot{j_}")
    P.record["c_wot"] = P.record["c_wot1"]
    LD(s_wp[1][0:1, :], b_out.rearrange("(a d) -> a d", a=1), "ld_bo", waits=list(p1_done))
    V(lambda e: e.tensor_scalar(s_boutr[:, :], s_wp[1][0:1, :], 0.25, None, op0=ALU.mult), ["ld_bo"], "c_boutr")
    s_sgs = [sb2("s_sgs0", [128, 1024], BF16)]
    s_jkw = sb2("s_jkw", [128, 1024], BF16)
    s_sg2 = s_jkw

    prep_jobs = []
    prep_deferred = []
    pj = [0]

    s_gp = mksem("s_gp")

    def prep_alpha1(ti):
        j = pj[0]; pj[0] += 1
        w = [f"pj_{j-2}"] if j >= 2 else list(p1_done) + b_ready + ["c_wot1", "c_boutr"]
        ldpool.issue(s_wp[j % 2][:, :], w1[ti * 128:(ti + 1) * 128, :], f"pl_{j}", w)
        def fin(ti=ti, j=j):
            A_(lambda e, ti=ti, j=j: e.activation(s_jkw[:, :], s_wp[j % 2][:, :], AF.Abs, bias=0.0, scale=1.0,
                                                  accum_out=s_al1[:, ti:ti + 1]),
               [f"pl_{j}"] + ([f"pj_{j-1}"] if j >= 1 else []), f"pj_{j}")
        return fin

    def prep_alpha2(ti, qq_):
        j = pj[0]; pj[0] += 1
        w = [f"pj_{j-2}"] if j >= 2 else []
        ldpool.issue(s_wp[j % 2][:, :], w2[ti * 128:(ti + 1) * 128, qq_ * 1024:(qq_ + 1) * 1024], f"pl_{j}", w)
        def fin(ti=ti, qq_=qq_, j=j):
            A_(lambda e, ti=ti, qq_=qq_, j=j: e.activation(
                s_jkw[:, :], s_wp[j % 2][:, :], AF.Abs, bias=0.0, scale=1.0,
                accum_out=(s_al2[:, 8 + (qq_ % 2):9 + (qq_ % 2)] if qq_ > 0 else s_al2[:, ti:ti + 1])),
               [f"pl_{j}", f"pj_{j-1}"] + ([f"pa2_{ti}_{qq_-2}"] if qq_ >= 3 else []) +
               ([f"pa2_{ti-1}_{3}"] if (qq_ > 0 and ti >= 1) else []), f"pj_{j}")
            if qq_ > 0:
                V(lambda e, ti=ti, qq_=qq_: e.tensor_tensor(s_al2[:, ti:ti + 1], s_al2[:, ti:ti + 1],
                                                            s_al2[:, 8 + (qq_ % 2):9 + (qq_ % 2)], op=ALU.add),
                  [f"pj_{j}"] + ([f"pa2_{ti}_{qq_-1}"] if qq_ > 1 else []), f"pa2_{ti}_{qq_}")
        return fin

    sj_ctr = [0]

    def prep_sign(src_t, dst_img, r0, c0, nr, ncq):
        j = pj[0]; pj[0] += 1
        sj = sj_ctr[0]; sj_ctr[0] += 1
        w = [f"pj_{j-2}"] if j >= 2 else []
        ldpool.issue(s_wp[j % 2][:, 0:ncq], src_t[r0:r0 + nr, c0:c0 + ncq], f"pl_{j}", w)
        def fin(j=j, sj=sj, ncq=ncq):
            P.op("gpsimd", lambda e, j=j, sj=sj, ncq=ncq: e.tensor_scalar(
                s_sg2[:, 0:ncq], s_wp[j % 2][:, 0:ncq], 1.0e19, 1.0e19, op0=ALU.mult, op1=ALU.mult),
                [f"pl_{j}"] + ([f"pg2_{sj-1}"] if sj >= 1 else [f"pj_{j-1}"]), s_gp, 1, f"pg1_{sj}")
            P.op("gpsimd", lambda e, j=j, sj=sj, ncq=ncq: e.tensor_scalar(
                s_sgs[0][:, 0:ncq], s_sg2[:, 0:ncq], 1.0, -1.0, op0=getattr(ALU, "min"), op1=ALU.max),
                [f"pg1_{sj}"] + ([f"ps_{sj-1}"] if sj >= 1 else []), s_gp, 1, f"pg2_{sj}")
            P.record[f"pj_{j}"] = P.record[f"pg2_{sj}"]
            ST(dst_img[r0:r0 + nr, c0:c0 + ncq], s_sgs[0][:, 0:ncq], f"ps_{sj}", waits=[f"pg2_{sj}"])
        return fin

    for ti in range(32):
        prep_jobs.append(lambda ti=ti: prep_alpha1(ti))
    for ti in range(8):
        for qq_ in range(4):
            prep_jobs.append(lambda ti=ti, qq_=qq_: prep_alpha2(ti, qq_))
    for ti in range(8):
        for qq_ in range(4):
            prep_jobs.append(lambda ti=ti, qq_=qq_: prep_sign(w1t, sg1d, ti * 128, qq_ * 1024, 128, 1024))
    for ti in range(32):
        prep_jobs.append(lambda ti=ti: prep_sign(w2t, sg2d, ti * 128, 0, 128, 1024))

    if NQ % 4 == 0:
        TQORDER = [t for j in reversed(range(4)) for t in range(NQ) if t % 4 == j]
    else:
        TQORDER = list(reversed(range(NQ)))
    pairs = [(h, tq) for tq in TQORDER for h in range(NH)]
    NP = len(pairs)

    def batches_of(tq):
        out = []
        full = list(range(4 * tq))
        for i in range(0, len(full), 3):
            out.append([(kt, 512, 512 * j, 0) for j, kt in enumerate(full[i:i + 3])])
        out.append([(4 * tq + 0, 512, 0, 0), (4 * tq + 1, 384, 512, 128), (4 * tq + 2, 256, 1024, 256)])
        out.append([(4 * tq + 3, 128, 0, 384)])
        return out

    spm_hist = []
    spm_last = {}
    rs_pending = []
    rs_emitted = set()
    CH4 = seq // 4
    SW = max(seq // 16, 128)
    NCH = CH4 // SW
    chunk_of = lambda tq, c: ((tq * 512 + c * 128) % CH4) // SW

    def emit_spm_one(pos, c):
        tq = TQORDER[pos]
        p0 = 4 * pos
        bi = bi_g[0]
        bi_g[0] += 1
        fi = len(spm_hist)
        spm_hist.append((tq, c))
        w0 = [f"astfe_{p0 + hh}" for hh in range(4)] + ["c_wot", "c_boutr"]
        if bi >= 2:
            w0.append(f"exp_{bi-2}")
        first = [True]
        for dh2 in range(2):
            for hpair in range(2):
                PE(lambda e, bi=bi, c=c, dh2=dh2, hpair=hpair, pos=pos: e.matmul(
                    p_sy[bi % 2][:, dh2 * 512:(dh2 + 1) * 512],
                    s_astF2[pos % 2][:, hpair * 512 + c * 128: hpair * 512 + (c + 1) * 128],
                    WoT2[:, hpair * D + dh2 * 512: hpair * D + dh2 * 512 + 512],
                    start=(hpair == 0), stop=False, skip_group_check=True),
                    w0 if first[0] else [], None)
                first[0] = False
        for dh2 in range(2):
            PE(lambda e, bi=bi, dh2=dh2: e.matmul(
                p_sy[bi % 2][:, dh2 * 512:(dh2 + 1) * 512], s_onesb[:, :],
                s_boutr[:, dh2 * 512:(dh2 + 1) * 512],
                start=False, stop=True, skip_group_check=True),
                [], f"sy_{bi}" if dh2 == 1 else None)
        wv = [f"sy_{bi}"]
        if fi >= 2:
            pt, pc = spm_hist[fi - 2]
            wv.append(f"sps_{pt}_{pc}")
        V(lambda e, bi=bi, fi=fi: e.tensor_copy(s_sp[fi % 2][:, :], p_sy[bi % 2][:, 0:D]),
          wv, f"spv_{bi}")
        P.record[f"exp_{bi}"] = P.record[f"spv_{bi}"]
        P.record[f"avb_{bi}"] = P.record[f"avb_{bi-3}"] if bi >= 3 else P.record[f"spv_{bi}"]
        t0 = tq * 512 + c * 128
        qq_, wi = t0 // CH4, t0 % CH4
        jj_, ii = wi // SW, wi % SW
        row0 = jj_ * (4 * SW) + qq_ * SW + ii
        ST(spart[row0:row0 + 128, :], s_sp[fi % 2][:, :], f"sps_{tq}_{c}", waits=[f"spv_{bi}"])
        if c == 3:
            spm_last[pos] = f"sy_{bi}"
        done_tc = set(spm_hist)
        for jc in range(NCH):
            if jc in rs_emitted:
                continue
            need = [(t, cc) for t in range(NQ) for cc in range(4) if chunk_of(t, cc) == jc]
            if all(tc in done_tc for tc in need):
                rs_emitted.add(jc)
                wj = [f"sps_{t}_{cc}" for (t, cc) in need]
                if jc == 0:
                    rs_pending.append((jc, wj))
                    continue
                P.op("gpsimd", lambda e, jc=jc: e.collective_compute(
                    "ReduceScatter", ALU.add, replica_groups=[[0, 1, 2, 3], [4, 5, 6, 7]],
                    ins=[spart[jc * 4 * SW:(jc + 1) * 4 * SW, :].opt()],
                    outs=[sown[jc * SW:(jc + 1) * SW, :].opt()]),
                    waits=wj, inc=s_cc, inck=1, ms=f"rs{jc}_done")

    def emit_diag(p):
        h, tq = pairs[p]
        for j in range(4):
            t = 4 * tq + j
            w = []
            if j == 0:
                w = ([f"astfe_{p-2}"] if p >= 2 else list(p1_done)) + [f"b70_{h}"]
            PE(lambda e, h=h, t=t, j=j, p=p: e.matmul(
                p_dg_of(p)[:, j * 128:(j + 1) * 128],
                QA[h][0:64, t * 128:(t + 1) * 128], KA[h][0:64, t * 128:(t + 1) * 128],
                start=True, stop=True), w, f"dg_{p}_{j}")

    def emit_mstats(p):
        h, tq = pairs[p]
        for j in range(4):
            w = [f"dg_{p}_{3}", "ld_mx", "c_m01"]
            if j >= 1:
                w.append(f"ttr_{p}_{j-1}")
            elif p >= 1:
                w.append(f"ttm_{p-1}")
            V(lambda e, j=j, p=p: e.tensor_tensor(s_dj[:, :], p_dg_of(p)[:, j * 128:(j + 1) * 128], s_mxn[:, :],
                                                  op=ALU.add), w, f"ttra_{p}_{j}")
            V(lambda e, j=j: e.tensor_scalar(s_dj2[:, :], s_dj[:, :], -1.0, None, op0=ALU.mult,
                                             op1=ALU_MIN, accum_out=s_negL[:, j:j + 1]),
              [f"ttra_{p}_{j}"], f"ttr_{p}_{j}")
        V(lambda e, h=h, tq=tq: e.tensor_tensor(s_negm[:, :], s_negL[:, :],
                                                B70N[:, h * T + 4 * tq: h * T + 4 * tq + 4], op=ALU_MIN),
          [f"ttr_{p}_{j}" for j in range(4)] + [f"b70_{h}"], f"ttm_{p}")

    def emit_mtrans(p):
        for j in range(4):
            PE(lambda e, j=j, p=p: e.transpose(p_dg_of(p)[0:1, j * 128:(j + 1) * 128], s_negm[:, j:j + 1],
                                               s_id[:, :]),
               [f"ttm_{p}"] + [f"ttr_{p}_{jj}" for jj in range(4)], f"mtr_{p}_{j}")
        h, tq = pairs[p]
        for j in range(4):
            V(lambda e, h=h, tq=tq, j=j, p=p: e.tensor_copy(
                QA[h][64:65, (4 * tq + j) * 128:(4 * tq + j) * 128 + 128],
                p_dg_of(p)[0:1, j * 128:(j + 1) * 128]),
                [f"mtr_{p}_{3}"], f"ngc_{p}_{j}")

    bi_g = [0]

    def emit_sweep(p, inj=None):
        inj = inj or {}
        h, tq = pairs[p]
        bs = batches_of(tq)
        nb = len(bs)
        base = bi_g[0]
        for b in range(nb + 1):
            bi = base + b
            cb = inj.pop(b, None) or (inj.pop(min(inj), None) if (b == nb and inj) else None)
            if cb is not None:
                cb()
            if b == nb and inj:
                for k in sorted(inj):
                    inj.pop(k)()
            if b < nb:
                batch = bs[b]
                w = [f"ngc_{p}_{j}" for j in range(4)]
                if bi >= 2:
                    w.append(f"exp_{bi-2}")
                last = len(batch) - 1
                for i, (kt, wd, off, qoff) in enumerate(batch):
                    PE(lambda e, h=h, tq=tq, kt=kt, wd=wd, off=off, qoff=qoff, bi=bi: e.matmul(
                        p_sy[bi % 2][:, off:off + wd],
                        KA[h][:, kt * 128:(kt + 1) * 128],
                        QA[h][:, tq * 512 + qoff: tq * 512 + qoff + wd],
                        start=True, stop=True),
                        w if i == 0 else [], f"sy_{bi}" if i == last else None)
                we = [f"sy_{bi}"]
                if bi >= 3:
                    we.append(f"avb_{bi-3}")
                wexp = max(off + wd for (kt, wd, off, qoff) in batch)
                A_(lambda e, bi=bi, wexp=wexp: e.activation(PT[bi % 3][:, 0:wexp], p_sy[bi % 2][:, 0:wexp],
                                                            AF.Exp, bias=0.0, scale=1.0),
                   we, f"exp_{bi}")
                if b >= nb - 2:
                    for i, (kt, wd, off, qoff) in enumerate(batch):
                        V(lambda e, bi=bi, off=off: e.tensor_tensor(
                            PT[bi % 3][:, off:off + 128], PT[bi % 3][:, off:off + 128], s_m01[:, :], op=ALU.mult),
                            [f"exp_{bi}", "c_m01"], f"msk_{bi}_{i}")
            if b > 0:
                bb = b - 1
                bi2 = base + bb
                batch = bs[bb]
                w = [f"exp_{bi2}"]
                if bb >= nb - 2:
                    w += [f"msk_{bi2}_{i}" for i in range(len(batch))]
                if bb == 0:
                    w.append(f"oev_{p-2}" if p >= 2 else "c_va1")
                    if p >= 2:
                        w.append(f"astfe_{p-2}")
                last = len(batch) - 1
                for i, (kt, wd, off, qoff) in enumerate(batch):
                    PE(lambda e, h=h, kt=kt, wd=wd, off=off, qoff=qoff, bi2=bi2, p=p,
                       st_=(bb == 0 and i == 0), sp_=(bb == nb - 1 and i == last): e.matmul(
                        p_oa[p % 2][0:65, qoff:qoff + wd],
                        VA[:, (h * T + kt) * 65:(h * T + kt) * 65 + 65],
                        PT[bi2 % 3][:, off:off + wd],
                        start=st_, stop=sp_, skip_group_check=True),
                        w if i == 0 else [], f"avb_{bi2}" if i == last else None)
        bi_g[0] += nb

    def emit_epi_start(p):
        V(lambda e, p=p: e.tensor_copy(s_oT[0][:, :], p_oa[p % 2][0:65, :]),
          [f"avb_{bi_g[0]-1}"] + ([f"etr_{p-1}_{3}"] if p >= 1 else []), f"oev_{p}")

    def emit_epi_tr(p):
        for j in range(4):
            w = [f"oev_{p}"]
            if p >= 2 and j == 0:
                w.append(f"astfe_{p-2}")
            PE(lambda e, j=j, p=p: e.transpose(p_ep_of(p)[:, j * 65:(j + 1) * 65],
                                               s_oT[0][:, j * 128:(j + 1) * 128],
                                               s_id[0:65, 0:65]), w, f"etr_{p}_{j}")

    def emit_epi_fin(p):
        h, tq = pairs[p]
        for j in range(4):
            V(lambda e, j=j, p=p: e.reciprocal(s_rl[:, j:j + 1], p_ep_of(p)[:, j * 65 + 64:j * 65 + 65]),
              [f"etr_{p}_{3}"] + ([f"rcp_{p}_{j-1}"] if j >= 1 else []), f"rcp_{p}_{j}")
        for j in range(4):
            w = [f"rcp_{p}_{j}"]
            if p >= 2 and j == 0:
                w.append(f"astf_{p-2}_{3}")
                if "attn_q" in taps:
                    w.append(f"ast_{p-2}")
            V(lambda e, p=p, j=j: e.tensor_scalar(
                s_ast[p % 2][:, j * 64:(j + 1) * 64], p_ep_of(p)[:, j * 65:j * 65 + 64],
                s_rl[:, j:j + 1], None, op0=ALU.mult), w, f"nrm_{p}_{j}")
        if "attn_q" in taps:
            ST(attn_q[tq * 512:(tq + 1) * 512, h * 64:(h + 1) * 64].rearrange("(j p) c -> p j c", p=128),
               s_ast[p % 2][:, :].rearrange("p (j c) -> p j c", c=64),
               f"ast_{p}", waits=[f"nrm_{p}_{j}" for j in range(4)])
        pos = p // 4
        for j in range(4):
            w = [f"nrm_{p}_{jj}" for jj in range(4)] if j == 0 else []
            PE(lambda e, j=j, p=p: e.matmul(
                p_ep_of(p)[0:64, j * 128:(j + 1) * 128],
                s_ast[p % 2][:, j * 64:(j + 1) * 64], s_idb[:, :],
                start=True, stop=True, skip_group_check=True), w, f"astf_{p}_{j}")
        wv = [f"astf_{p}_{3}"]
        if pos - 2 in spm_last:
            wv.append(spm_last[pos - 2])
        V(lambda e, p=p, h=h, pos=pos: e.tensor_copy(
            s_astF2[pos % 2][(h % 2) * 64:(h % 2) * 64 + 64, (h // 2) * 512:(h // 2) * 512 + 512],
            p_ep_of(p)[0:64, 0:512]), wv, f"astfe_{p}")

    emit_diag(0)
    emit_mstats(0)
    emit_mtrans(0)
    for p in range(NP):
        hh_, pos_ = p % 4, p // 4
        if hh_ == 1 and pos_ >= 2:
            emit_spm_one(pos_ - 2, 3)
        if hh_ >= 1 and pos_ >= 1:
            emit_spm_one(pos_ - 1, hh_ - 1)
        inj = {}
        if p >= 1:
            inj[1] = lambda p=p: (emit_epi_tr(p - 1), emit_epi_fin(p - 1))
        if p + 1 < NP:
            inj[2] = lambda p=p: (emit_diag(p + 1), emit_mstats(p + 1))
        emit_sweep(p, inj)
        for cb_ in prep_deferred:
            cb_()
        prep_deferred.clear()
        for _ in range(2):
            if prep_jobs:
                fin_ = prep_jobs.pop(0)()
                if fin_:
                    prep_deferred.append(fin_)
        if p + 1 < NP:
            emit_mtrans(p + 1)
        emit_epi_start(p)
    emit_epi_tr(NP - 1)
    emit_epi_fin(NP - 1)
    if len(TQORDER) >= 2:
        emit_spm_one(len(TQORDER) - 2, 3)
    for c in range(4):
        emit_spm_one(len(TQORDER) - 1, c)
    assert rs_emitted == set(range(NCH))
    for cb_ in prep_deferred:
        cb_()
    prep_deferred.clear()
    while prep_jobs:
        fin_ = prep_jobs.pop(0)()
        if fin_:
            fin_()
    V(lambda e: e.tensor_scalar(s_al1[:, :], s_al1[:, :], 1.0 / D, None, op0=ALU.mult),
      [f"pj_{pj[0]-1}"] + [f"pa2_{ti}_{3}" for ti in range(8)], "al1s")
    V(lambda e: e.tensor_scalar(s_al2[:, 0:8], s_al2[:, 0:8], 1.0 / FFH, None, op0=ALU.mult),
      ["al1s"], "al2s")
    prep_done = ["al1s", "al2s"] + [f"ps_{sj}" for sj in range(sj_ctr[0])]
    sp_done = [f"sps_{t}_{c}" for t in range(NQ) for c in range(4)]

    tap_specs = [("KA0", KA[0][:, :], [65, seq], BF16), ("QA0", QA[0][:, :], [65, seq], BF16),
                 ("VA", VA[:, :], [128, NH * T * 65], BF16), ("B70N", B70N[:, :], [128, NH * T], BF16)]
    tap_ms = []
    for name, ap, shape, dt in tap_specs:
        if name in taps:
            dst = nc.dram_tensor("tap_" + name, shape, dt, kind="ExternalOutput").ap()
            ST(dst, ap, f"tap_{name}", waits=list(sp_done) + [f"ngc_{NP-1}_{3}"])
            tap_ms.append(f"tap_{name}")

    es2.close()
    es_attn.close()

    es3 = contextlib.ExitStack()
    sb3 = lambda n, s, d: es3.enter_context(nc.sbuf_tensor(n, s, d))
    ps3 = lambda n, s, d: es3.enter_context(nc.psum_tensor(n, s, d))
    s_s1c = [sb3(f"s_s1c{i}", [128, 8 * 256], BF16) for i in range(4)]
    s_s2c = [sb3(f"s_s2c{i}", [128, 32 * 256], BF16) for i in range(4)]
    s_xg = [sb3(f"s_xg{i}", [128, D], F32) for i in range(4)]
    s_xn = [sb3(f"s_xn{i}", [128, D], BF16) for i in range(4)]
    xnT2 = [sb3(f"xnT{i}", [128, 8 * GSZ], BF16) for i in range(2)]
    s_actb2 = [sb3(f"s_actb{i}", [128, 32 * GSZ], BF16) for i in range(2)]
    s_gf = [sb3(f"s_gf{i}", [128, GSZ], F32) for i in range(2)]
    s_stm = [sb3(f"s_stm{i}", [128, GT * 128], F32) for i in range(2)]
    s_xtm = [sb3(f"s_xtm{i}", [128, GT * 128], F32) for i in range(2)]
    s_gtm = [sb3(f"s_gtm{i}", [128, GT * 128], F32) for i in range(2)]
    s_otm = [sb3(f"s_otm{i}", [128, GT * 128], F32) for i in range(2)]
    p_mm = [ps3(f"p_mm{i}", [128, 512], F32) for i in range(4)]
    p_t3 = [ps3(f"p_t3{i}", [128, 1024], BF16) for i in range(2)]
    p_g3 = [ps3(f"p_g3{i}", [128, 512], F32) for i in range(2)]

    barrier3 = list(sp_done) + list(tap_ms) + prep_done

    def ln3(g):
        pref = f"l3_{g}_"
        pprev = f"l3_{g-1}_"
        lw = lambda it, g=g: ([f"{pprev}a{it}"] if g >= 1 else (list(barrier3) if it == 0 else []))
        aw = lambda it, g=g: ([f"xt_{(g-1)*GT+it}_7"] if g >= 1 else [])
        ln_group(pref, GT, lambda it: s_xg[it], lambda it: s_xn[it],
                 lambda it, g=g: x_own[((OG - 1 - g) * GT + it) * 128:((OG - 1 - g) * GT + it + 1) * 128, :],
                 lw, aw, False, 48 * (g % 2),
                 extra_first=([f"l3_{g-2}_a{it}" for it in range(GT)] if g >= 2 else []),
                 jk_war=[f"{pprev}sq{GT-1}"] if g >= 1 else [])

    ln3(0)
    for g in range(OG):
        if g + 1 < OG:
            pass
        for it in range(GT):
            t = g * GT + it
            for dc in range(8):
                w = [f"l3_{g}_a{it}", "c_idb"]
                if t >= 2 and dc == 0:
                    w.append(f"xte_{t-2}_{1}")
                if dc == 4:
                    w.append(f"xte_{t}_{0}")
                PE(lambda e, t=t, it=it, dc=dc: e.transpose(
                    p_t3[t % 2][:, (dc % 4) * 128:(dc % 4) * 128 + 128],
                    s_xn[it % 4][:, dc * 128:(dc + 1) * 128], s_idb[:, :]), w, f"xt_{t}_{dc}")
                if dc == 3 or dc == 7:
                    half = dc // 4
                    wv = [f"xt_{t}_{half*4+j}" for j in range(4)]
                    if g >= 2 and it == 0 and half == 0:
                        wv.append(f"mm1_{g-2}_{31}")
                    V(lambda e, t=t, it=it, half=half, g=g: e.tensor_copy(
                        xnT2[g % 2][:, :].rearrange("p (dd c) -> p dd c", dd=8)[:, half * 4:(half + 1) * 4,
                                                                               it * 128:(it + 1) * 128],
                        p_t3[t % 2][:, 0:512].rearrange("p (k c) -> p k c", c=128)), wv, f"xte_{t}_{half}")
        xnT_ready = [f"xte_{g*GT+it}_{h}" for it in range(GT) for h in range(2)]
        if g + 1 < OG:
            ln3(g + 1)
        for hcb in range(16):
            if hcb >= 4:
                wl = [f"gel_{g}_{2*(hcb-4)+1}"]
            elif g >= 1:
                wl = [f"gel_{g-1}_{2*(hcb+12)+1}"]
            else:
                wl = list(barrier3)
            ldpool.issue(s_s1c[hcb % 4][:, :].rearrange("p (dt c) -> p dt c", c=256),
                         sg1d[:, hcb * 256:(hcb + 1) * 256].rearrange("(dt p) c -> p dt c", p=128),
                         f"s1l_{g}_{hcb}", wl)
            for hc2 in range(2):
                hc = hcb * 2 + hc2
                pb = p_mm[hc2]
                for dc in range(8):
                    w = []
                    if dc == 0:
                        w = list(xnT_ready) + [f"s1l_{g}_{hcb}"]
                        if hc >= 2:
                            w.append(f"gel_{g}_{hc-2}")
                        elif g >= 1:
                            w.append(f"gel_{g-1}_{hc+30}")
                    PE(lambda e, hcb=hcb, hc2=hc2, dc=dc, pb=pb, g=g: e.matmul(
                        pb[:, 0:GSZ], s_s1c[hcb % 4][:, dc * 256 + hc2 * 128: dc * 256 + hc2 * 128 + 128],
                        xnT2[g % 2][:, :].rearrange("p (dd c) -> p dd c", dd=8)[:, dc, 0:GSZ],
                        start=(dc == 0), stop=(dc == 7)),
                        w, f"mm1_{g}_{hc}" if dc == 7 else None)
                w = [f"mm1_{g}_{hc}", "c_b1fm", "al1s"]
                if g >= 2 and hc == 0:
                    w.append(f"mm2_{g-2}_{7}")
                A_(lambda e, g=g, hc=hc, hc2=hc2: e.activation(
                    s_actb2[g % 2][:, hc * GSZ:(hc + 1) * GSZ], p_mm[hc2][:, 0:GSZ],
                    AF.Gelu, bias=s_b1fm[:, hc:hc + 1], scale=s_al1[:, hc:hc + 1]), w, f"gel_{g}_{hc}")
        if g == 0:
            for jc_, wj_ in rs_pending:
                P.op("gpsimd", lambda e, jc=jc_: e.collective_compute(
                    "ReduceScatter", ALU.add, replica_groups=[[0, 1, 2, 3], [4, 5, 6, 7]],
                    ins=[spart[jc * 4 * SW:(jc + 1) * 4 * SW, :].opt()],
                    outs=[sown[jc * SW:(jc + 1) * SW, :].opt()]),
                    waits=wj_ + [f"mm1_{0}_{15}"], inc=s_cc, inck=1, ms=f"rs{jc_}_done")
            rs_pending.clear()
        for dcb in range(4):
            wl = [f"sil_{g-1}_{2*dcb+1}"] if g >= 1 else [f"gel_{g}_{0}"]
            ldpool.issue(s_s2c[dcb % 4][:, :].rearrange("p (ht c) -> p ht c", c=256),
                         sg2d[:, dcb * 256:(dcb + 1) * 256].rearrange("(ht p) c -> p ht c", p=128),
                         f"s2l_{g}_{dcb}", wl)
            for dc2 in range(2):
                dc = dcb * 2 + dc2
                pf = p_mm[2 + dc2]
                for hc in range(32):
                    w = []
                    if hc == 0:
                        w = [f"gel_{g}_{hh}" for hh in range(32)] + [f"s2l_{g}_{dcb}"]
                        if dc >= 2:
                            w.append(f"sil_{g}_{dc-2}")
                    PE(lambda e, dcb=dcb, dc2=dc2, hc=hc, pf=pf, g=g: e.matmul(
                        pf[:, 0:GSZ], s_s2c[dcb % 4][:, hc * 256 + dc2 * 128: hc * 256 + dc2 * 128 + 128],
                        s_actb2[g % 2][:, hc * GSZ:(hc + 1) * GSZ],
                        start=(hc == 0), stop=(hc == 31)), w, f"mm2_{g}_{dc}" if hc == 31 else None)
                w = [f"mm2_{g}_{dc}", "c_b2fm", "al2s"]
                if dc >= 2:
                    w.append(f"gtr_{g}_{dc-2}_{GT-1}")
                elif g >= 1:
                    w.append(f"gtr_{g-1}_{dc+6}_{GT-1}")
                A_(lambda e, g=g, dc=dc, dc2=dc2: e.activation(
                    s_gf[dc % 2][:, :], p_mm[2 + dc2][:, 0:GSZ], AF.Silu,
                    bias=s_b2fm[:, dc:dc + 1], scale=s_al2[:, dc:dc + 1]), w, f"sil_{g}_{dc}")
                for st in range(GT):
                    w = [f"sil_{g}_{dc}"] if st == 0 else []
                    if st == 0 and (dc >= 2 or g >= 1):
                        w.append(f"gte_{g}_{dc-2}" if dc >= 2 else f"gte_{g-1}_{dc+6}")
                    PE(lambda e, dc=dc, st=st: e.transpose(
                        p_g3[dc % 2][:, st * 128:(st + 1) * 128], s_gf[dc % 2][:, st * 128:(st + 1) * 128],
                        s_id[:, :]), w, f"gtr_{g}_{dc}_{st}")
                w = [f"gtr_{g}_{dc}_{GT-1}"]
                if dc >= 2:
                    w.append(f"cm1_{g}_{dc-2}")
                elif g >= 1:
                    w.append(f"cm1_{g-1}_{dc+6}")
                V(lambda e, g=g, dc=dc: e.tensor_copy(s_gtm[dc % 2][:, :], p_g3[dc % 2][:, 0:GT * 128]), w,
                  f"gte_{g}_{dc}")
                gm = OG - 1 - g
                SWm = max(seq // 16, 128)
                w = [f"rs{j}_done" for j in range((gm * GSZ) // SWm, (gm * GSZ + GSZ - 1) // SWm + 1)]
                if dc >= 2:
                    w.append(f"cm1_{g}_{dc-2}")
                elif g >= 1:
                    w.append(f"cm1_{g-1}_{dc+6}")
                stpool.issue(s_stm[dc % 2][:, :].rearrange("p (s c) -> p s c", c=128),
                             sown[gm * GSZ:(gm + 1) * GSZ, dc * 128:(dc + 1) * 128].rearrange("(s p) c -> p s c", p=128),
                             f"sld_{g}_{dc}", w)
                w2_ = [f"cmb_{g}_{dc-2}"] if dc >= 2 else ([f"cmb_{g-1}_{dc+6}"] if g >= 1 else [])
                stpool.issue(s_xtm[dc % 2][:, :].rearrange("p (s c) -> p s c", c=128),
                             x_own[gm * GSZ:(gm + 1) * GSZ, dc * 128:(dc + 1) * 128].rearrange("(s p) c -> p s c", p=128),
                             f"xld_{g}_{dc}", w2_)
                w = [f"gte_{g}_{dc}", f"sld_{g}_{dc}"]
                if dc >= 2:
                    w.append(f"yst_{g}_{dc-2}")
                elif g >= 1:
                    w.append(f"yst_{g-1}_{dc+6}")
                V(lambda e, dc=dc: e.tensor_tensor(s_otm[dc % 2][:, :], s_stm[dc % 2][:, :],
                                                   s_gtm[dc % 2][:, :], op=ALU.mult), w, f"cm1_{g}_{dc}")
                V(lambda e, dc=dc: e.tensor_tensor(s_otm[dc % 2][:, :], s_otm[dc % 2][:, :],
                                                   s_xtm[dc % 2][:, :], op=ALU.add),
                  [f"cm1_{g}_{dc}", f"xld_{g}_{dc}"], f"cmb_{g}_{dc}")
                ST(y[gm * GSZ:(gm + 1) * GSZ, dc * 128:(dc + 1) * 128].rearrange("(s p) c -> p s c", p=128),
                   s_otm[dc % 2][:, :].rearrange("p (s c) -> p s c", c=128),
                   f"yst_{g}_{dc}", waits=[f"cmb_{g}_{dc}"])

    y_done = [f"yst_{g}_{dc}" for g in range(OG) for dc in range(8)]

    final_waits = y_done + [(s, s.n) for s in stpool.sems]
    P.op("sync", lambda e: e.nop(), final_waits, None)

    es3.close()

    with nc.Block() as block:
        @block.sync
        def _(eng):
            P.emit_engine("sync", eng)

        @block.vector
        def _(eng):
            P.emit_engine("vector", eng)

        @block.scalar
        def _(eng):
            P.emit_engine("scalar", eng)

        @block.tensor
        def _(eng):
            P.emit_engine("tensor", eng)

        @block.gpsimd
        def _(eng):
            P.emit_engine("gpsimd", eng)

    es.close()
    return nc


def host_inputs(inputs, seq):
    x = np.ascontiguousarray(np.asarray(inputs["L0_attn_x"], dtype=np.float32))
    wqkv_full = np.asarray(inputs["L0_attn_w_qkv"], dtype=np.float32)
    wout = np.asarray(inputs["L0_attn_w_out"], dtype=np.float32)
    OWN = seq // 4
    ident = np.eye(128, dtype=np.float32)
    qi, ki = np.arange(128)[:, None], np.arange(128)[None, :]
    maskxneg = np.where(ki > qi, np.float32(NEG_BIG), np.float32(0.0))
    mask01 = np.where(np.arange(128)[:, None] <= np.arange(128)[None, :],
                      np.float32(1.0), np.float32(0.0))
    shared = dict(
        w1=np.ascontiguousarray(np.asarray(inputs["L0_ffn_w1"], dtype=np.float32)),
        w2=np.ascontiguousarray(np.asarray(inputs["L0_ffn_w2"], dtype=np.float32)),
        w1t=np.ascontiguousarray(np.asarray(inputs["L0_ffn_w1"], dtype=np.float32).T),
        w2t=np.ascontiguousarray(np.asarray(inputs["L0_ffn_w2"], dtype=np.float32).T),
        gamma=np.ascontiguousarray(np.asarray(inputs["L0_attn_ln_gamma"], dtype=np.float32)),
        beta=np.ascontiguousarray(np.asarray(inputs["L0_attn_ln_beta"], dtype=np.float32)),
        b_out=np.ascontiguousarray(np.asarray(inputs["L0_attn_b_out"], dtype=np.float32)),
        b1=np.ascontiguousarray(np.asarray(inputs["L0_ffn_b1"], dtype=np.float32)),
        b2=np.ascontiguousarray(np.asarray(inputs["L0_ffn_b2"], dtype=np.float32)),
        ident=ident, maskxneg=maskxneg, mask01=mask01,
    )
    in_maps = []
    for c in range(8):
        b, hg = c // 4, c % 4
        sl = slice(256 * hg, 256 * hg + 256)
        wq = np.ascontiguousarray(np.concatenate(
            [wqkv_full[0:1024][sl], wqkv_full[1024:2048][sl], wqkv_full[2048:3072][sl]], axis=0))
        in_maps.append(dict(
            x_seq=np.ascontiguousarray(x[b]),
            x_own=np.ascontiguousarray(x[b, hg * OWN:(hg + 1) * OWN]),
            wqkv=wq,
            woutT=np.ascontiguousarray(wout[:, sl].T),
            **shared,
        ))
    return in_maps


def kernel(**inputs):
    seq = int(np.asarray(inputs["L0_attn_x"]).shape[1])
    nc = build(seq=seq)
    in_maps = host_inputs(inputs, seq)
    res = run_bass_kernel_spmd(nc, in_maps, core_ids=list(range(8)))
    OWN = seq // 4
    out = np.empty((2, seq, D), dtype=np.float32)
    for c in range(8):
        b, hg = c // 4, c % 4
        out[b, hg * OWN:(hg + 1) * OWN] = res.results[c]["y"]
    return out
